# version 14
# baseline (speedup 1.0000x reference)
import contextlib
import numpy as np
import ml_dtypes
import concourse.bass as bass
import concourse.mybir as mybir
from concourse.bass_utils import run_bass_kernel_spmd
from concourse.alu_op_type import AluOpType as ALU

F32 = mybir.dt.float32
BF16 = mybir.dt.bfloat16
AF = mybir.ActivationFunctionType

N_CORES = 8
B, SEQ, D = 32, 2048, 1024
NSEQ = B // N_CORES
NCHUNK = SEQ // 128
NTILE = NSEQ * NCHUNK
HEADS, HD = 16, 64
WIDTH = 4096
EPS = 1e-5


class Tracker:

    def __init__(self, plan=None):
        self.sems = {}
        self.cnt = {}
        self.m = {}
        self.last_wait = {}
        self.plan = plan

    def add_sem(self, name, handle):
        self.sems[name] = handle
        self.cnt[name] = 0

    def inc(self, sem_name, instr, n=1, tag=None):
        instr.then_inc(self.sems[sem_name], n)
        self.cnt[sem_name] += n
        if tag is not None:
            self.m[tag] = (sem_name, self.cnt[sem_name])
            if self.plan is not None:
                assert self.plan.get(tag) == (sem_name, self.cnt[sem_name]), \
                    f"plan divergence at {tag}"
        return self.cnt[sem_name]

    def mark(self, tag, sem_name):
        self.m[tag] = (sem_name, self.cnt[sem_name])
        if self.plan is not None:
            assert self.plan.get(tag) == (sem_name, self.cnt[sem_name]), \
                f"plan divergence at {tag}"

    def wait(self, eng, eng_key, tag):
        src = self.plan if self.plan is not None else self.m
        if tag not in src:
            return
        sem_name, count = src[tag]
        if count <= 0:
            return
        key = (eng_key, sem_name)
        if self.last_wait.get(key, -1) >= count:
            return
        eng.wait_ge(self.sems[sem_name], count)
        self.last_wait[key] = count


def _emit(tr, debug_x1=False):
    nc = bass.Bass("TRN2", target_bir_lowering=False)

    x_d = nc.dram_tensor("x", [NSEQ, SEQ, D], F32, kind="ExternalInput").ap()
    wq_d = nc.dram_tensor("wq", [D, D], F32, kind="ExternalInput").ap()
    wk_d = nc.dram_tensor("wk", [D, D], F32, kind="ExternalInput").ap()
    wv_d = nc.dram_tensor("wv", [D, D], F32, kind="ExternalInput").ap()
    wo_d = nc.dram_tensor("wo", [D, D], F32, kind="ExternalInput").ap()
    wg1_d = nc.dram_tensor("wg1", [D, HD], F32, kind="ExternalInput").ap()
    wg2_d = nc.dram_tensor("wg2", [HD, D], F32, kind="ExternalInput").ap()
    lnw_d = nc.dram_tensor("lnw", [D], F32, kind="ExternalInput").ap()
    lnb_d = nc.dram_tensor("lnb", [D], F32, kind="ExternalInput").ap()
    w1_d = nc.dram_tensor("w1", [D, WIDTH], F32, kind="ExternalInput").ap()
    b1_d = nc.dram_tensor("b1", [WIDTH], F32, kind="ExternalInput").ap()
    w2_d = nc.dram_tensor("w2", [WIDTH, D], F32, kind="ExternalInput").ap()
    b2_d = nc.dram_tensor("b2", [D], F32, kind="ExternalInput").ap()
    cos_d = nc.dram_tensor("costab", [SEQ, HD], BF16, kind="ExternalInput").ap()
    sin_d = nc.dram_tensor("sintab", [SEQ, HD], BF16, kind="ExternalInput").ap()
    mask_d = nc.dram_tensor("mask4", [128, 512], BF16, kind="ExternalInput").ap()
    ident_d = nc.dram_tensor("ident", [128, 128], BF16, kind="ExternalInput").ap()
    ones_d = nc.dram_tensor("onesrow", [1, 128], F32, kind="ExternalInput").ap()
    out_d = nc.dram_tensor("out", [NSEQ, SEQ, D], F32, kind="ExternalOutput").ap()
    x1_kind = "ExternalOutput" if debug_x1 else "Internal"
    x1_d = nc.dram_tensor("x1scr", [NTILE, 128, D], BF16, kind=x1_kind).ap()

    es = contextlib.ExitStack()
    with es:
        for s in ["pe", "dve", "act", "pool", "dx", "dw", "dc", "dxo",
                  "dx1", "dout", "dw2"]:
            tr.add_sem(s, es.enter_context(nc.semaphore(s)))

        w1_sb = es.enter_context(nc.sbuf_tensor("w1_sb", [128, 8 * WIDTH], BF16))
        b1_sb = es.enter_context(nc.sbuf_tensor("b1_sb", [128, 32], F32))
        b2row_bc = es.enter_context(nc.sbuf_tensor("b2row_bc", [128, D], F32))
        ident_s = es.enter_context(nc.sbuf_tensor("ident_s", [128, 128], BF16))

        P0 = es.enter_context(nc.psum_tensor("P0", [128, 512], F32))
        P1 = es.enter_context(nc.psum_tensor("P1", [128, 512], F32))
        P2 = es.enter_context(nc.psum_tensor("P2", [128, 512], F32))
        P3 = es.enter_context(nc.psum_tensor("P3", [128, 512], F32))
        P4 = es.enter_context(nc.psum_tensor("P4", [128, 512], F32))
        P5 = es.enter_context(nc.psum_tensor("P5", [128, 512], F32))
        P6 = es.enter_context(nc.psum_tensor("P6", [128, 512], F32))
        PT = es.enter_context(nc.psum_tensor("PT", [128, 1024], BF16))

        block = es.enter_context(nc.Block())

        esA = contextlib.ExitStack()
        with esA:
            sb = lambda name, shape, dt=BF16: esA.enter_context(
                nc.sbuf_tensor(name, shape, dt))
            wq_sb = sb("wq_sb", [128, 8 * D])
            wk_sb = sb("wk_sb", [128, 8 * D])
            wv_sb = sb("wv_sb", [128, 8 * D])
            wo_sb = sb("wo_sb", [128, 8 * D])
            wg1_sb = sb("wg1_sb", [128, 8 * HD])
            wg2_sb = sb("wg2_sb", [64, D])
            cos_sb = sb("cos_sb", [128, NCHUNK * HD])
            sin_sb = sb("sin_sb", [128, NCHUNK * HD])
            mask_sb = sb("mask_sb", [128, 512])
            b1row_bc = sb("b1row_bc", [128, D], F32)

            esS = contextlib.ExitStack()
            sbS = lambda name, shape, dt=BF16: esS.enter_context(
                nc.sbuf_tensor(name, shape, dt))
            lnw_sb = sbS("lnw_sb", [128, 8], F32)
            lnb_sb = sbS("lnb_sb", [128, 8], F32)
            lnb_bf = sbS("lnb_bf", [128, 8])
            b1row_f = sbS("b1row_f", [1, D], F32)
            b2_row = sbS("b2_row", [1, D], F32)
            ones_s = sbS("ones_s", [1, 128], F32)

            @block.sync
            def _(sync):
                for src, dst in ((ident_d, ident_s), (mask_d, mask_sb)):
                    sync.dma_start(out=dst[:], in_=src).then_inc(tr.sems["dc"], 16)
                    tr.cnt["dc"] += 16
                sync.dma_start(out=ones_s[:], in_=ones_d).then_inc(tr.sems["dc"], 16)
                tr.cnt["dc"] += 16
                for src, dst in ((cos_d, cos_sb), (sin_d, sin_sb)):
                    sync.dma_start(
                        out=dst[:].rearrange("p (i d) -> p i d", i=NCHUNK),
                        in_=src.rearrange("(i p) d -> p i d", p=128),
                    ).then_inc(tr.sems["dc"], 16)
                    tr.cnt["dc"] += 16
                for src, dst in ((lnw_d, lnw_sb), (lnb_d, lnb_sb)):
                    for j2 in range(2):
                        for ho in range(2):
                            for j1 in range(4):
                                off = j1 * 256 + j2 * 128 + ho * 64
                                sync.dma_start(
                                    out=dst[64 * j2:64 * (j2 + 1),
                                            ho * 4 + j1:ho * 4 + j1 + 1],
                                    in_=src[off:off + 64].unsqueeze(1),
                                ).then_inc(tr.sems["dc"], 16)
                                tr.cnt["dc"] += 16
                sync.dma_start(out=b1_sb[:],
                               in_=b1_d.rearrange("(c p) -> p c", p=128),
                               allow_slow_non_contiguous=True
                               ).then_inc(tr.sems["dc"], 16)
                tr.cnt["dc"] += 16
                sync.dma_start(out=b2_row[:], in_=b2_d.unsqueeze(0)
                               ).then_inc(tr.sems["dc"], 16)
                tr.cnt["dc"] += 16
                tr.mark("consts", "dc")

            @block.gpsimd
            def _(gp):
                for wsrc, wdst, tag in ((wq_d, wq_sb, "w_q"), (wk_d, wk_sb, "w_k"),
                                        (wv_d, wv_sb, "w_v")):
                    for c in range(8):
                        gp.dma_start(out=wdst[:, D * c:D * (c + 1)],
                                     in_=wsrc[128 * c:128 * (c + 1), :]
                                     ).then_inc(tr.sems["dw"], 16)
                        tr.cnt["dw"] += 16
                    tr.mark(tag, "dw")
                for c in range(8):
                    ho, jj = divmod(c, 4)
                    for half in range(2):
                        h_old = 2 * (2 * jj + half) + ho
                        gp.dma_start(
                            out=wo_sb[64 * half:64 * (half + 1), D * c:D * (c + 1)],
                            in_=wo_d[64 * h_old:64 * (h_old + 1), :],
                        ).then_inc(tr.sems["dw"], 16)
                        tr.cnt["dw"] += 16
                tr.mark("w_o", "dw")
                gp.dma_start(out=wg1_sb[:].rearrange("p (c f) -> p c f", c=8),
                             in_=wg1_d.rearrange("(c p) f -> p c f", p=128)
                             ).then_inc(tr.sems["dw"], 16)
                tr.cnt["dw"] += 16
                tr.mark("w_g1", "dw")
                wg2_perm = wg2_d.rearrange("k (j ho d) -> k ho j d",
                                           j=8, ho=2, d=64)
                for ho in range(2):
                    gp.dma_start(
                        out=wg2_sb[:, 512 * ho:512 * (ho + 1)].rearrange(
                            "k (j d) -> k j d", j=8),
                        in_=wg2_perm[:, ho],
                    ).then_inc(tr.sems["dw"], 16)
                    tr.cnt["dw"] += 16
                tr.mark("w_g2", "dw")

            @block.vector
            def _(v):
                tr.wait(v, "dve", "consts")
                v.tensor_copy(lnb_bf[:], lnb_sb[:])
                tr.inc("dve", v.drain(), tag="lnb_bf")

            @block.tensor
            def _(pe):
                tr.wait(pe, "pe", "lnb_bf")
                tr.wait(pe, "pe", "w_o")
                for c in range(8):
                    for half in range(2):
                        ins = pe.matmul(
                            (P1 if half == 0 else P2)[0:1, :],
                            lnb_bf[:, c:c + 1],
                            wo_sb[:, D * c + 512 * half:D * c + 512 * (half + 1)],
                            start=(c == 0), stop=(c == 7))
                tr.inc("pe", ins, tag="b1row_mm")

            @block.scalar
            def _(a):
                tr.wait(a, "act", "b1row_mm")
                a.copy(b1row_f[0:1, 0:512], P1[0:1, :])
                a.copy(b1row_f[0:1, 512:1024], P2[0:1, :])
                tr.inc("act", a.drain(), tag="b1row_evac")

            @block.vector
            def _(v):
                tr.wait(v, "dve", "b1row_mm")
                for c in range(8):
                    ins = v.tensor_scalar(wo_sb[:, D * c:D * (c + 1)],
                                          wo_sb[:, D * c:D * (c + 1)],
                                          lnw_sb[:, c:c + 1], None, ALU.mult)
                tr.inc("dve", ins, tag="woprime")

            @block.tensor
            def _(pe):
                tr.wait(pe, "pe", "b1row_evac")
                pe.matmul(P3[:, :], ones_s[:], b1row_f[0:1, 0:512],
                          start=True, stop=True)
                pe.matmul(P4[:, :], ones_s[:], b1row_f[0:1, 512:1024],
                          start=True, stop=True)
                pe.matmul(P5[:, :], ones_s[:], b2_row[0:1, 0:512],
                          start=True, stop=True)
                ins = pe.matmul(P6[:, :], ones_s[:], b2_row[0:1, 512:1024],
                                start=True, stop=True)
                tr.inc("pe", ins, tag="bcast_mm")

            @block.scalar
            def _(a):
                tr.wait(a, "act", "bcast_mm")
                a.copy(b1row_bc[:, 0:512], P3[:, :])
                a.copy(b1row_bc[:, 512:1024], P4[:, :])
                a.copy(b2row_bc[:, 0:512], P5[:, :])
                tr.inc("act", a.copy(b2row_bc[:, 512:1024], P6[:, :]),
                       tag="bcast_evac")

            esS.close()

            xbuf = [sb(f"xA{i}", [128, D], F32) for i in range(4)]
            xn_s = sb("xn_s", [128, D])
            xnT_s = sb("xnT_s", [128, D])
            xnT_s2 = sb("xnT_s2", [128, D])
            q_tm = sb("q_tm", [128, D])
            k_tm = sb("k_tm", [128, D])
            qm1 = sb("qm1", [128, D])
            km1 = sb("km1", [128, D])
            qr_s = sb("qr_s", [128, D])
            kr_s = sb("kr_s", [128, D])
            qt_s = sb("qt_s", [128, D])
            kt_s = sb("kt_s", [128, D])
            v_s = sb("v_s", [128, D])
            g1t_s = sb("g1t_s", [64, 128])
            e_s = sb("e_s", [128, 2048])
            ssb_s = sb("ssb_s", [128, 512])
            gate_s = sb("gate_s", [128, D])
            z_s = sb("z_s", [128, D])
            zppT_s = sb("zppT_s", [128, D])
            y_s = sb("y_s", [128, D], F32)
            t_s = sb("t_s", [128, D], F32)
            x1_s = sb("x1_s", [128, D])
            lnbuf = {}
            for pfx in ("l1", "li", "lo"):
                lnbuf[pfx] = dict(
                    bna=sb(f"{pfx}_bna", [128, 12], F32),
                    bn2=sb(f"{pfx}_bn2", [128, 2], F32),
                    v=sb(f"{pfx}_v", [128, 1], F32),
                    sq=sb(f"{pfx}_sq", [128, 1], F32),
                    rstd=sb(f"{pfx}_rstd", [128, 1], F32),
                    nmu=sb(f"{pfx}_nmu", [128, 1], F32),
                )
            l1, li, lo = lnbuf["l1"], lnbuf["li"], lnbuf["lo"]
            dbg = {}
            if debug_x1:
                for nm, dt in [("dbg_x", F32), ("dbg_xn", BF16)]:
                    dbg[nm] = nc.dram_tensor(nm, [128, D], dt,
                                             kind="ExternalOutput").ap()

            @block.sync
            def _(sync):
                tr.wait(sync, "sync", "bcast_mm")
                tr.wait(sync, "sync", "b1row_evac")
                tr.wait(sync, "sync", "lnb_bf")
                tr.wait(sync, "sync", "woprime")
                for t in range(3):
                    sq, iq = divmod(t, NCHUNK)
                    sync.dma_start(out=xbuf[t % 4][:],
                                   in_=x_d[sq, 128 * iq:128 * (iq + 1), :]
                                   ).then_inc(tr.sems["dx"], 16)
                    tr.cnt["dx"] += 16
                    tr.mark(f"x_{t}", "dx")

            def ln_stats(v, buf, src_ap, tag, eps):
                v.bn_stats(buf["bna"][:, 0:6], src_ap[:, 0:512])
                v.bn_stats(buf["bna"][:, 6:12], src_ap[:, 512:1024])
                v.drain()
                v.bn_aggr(buf["bn2"][:], buf["bna"][:])
                v.drain()
                v.tensor_scalar(buf["v"][:], buf["bn2"][:, 1:2],
                                float(eps), None, ALU.add)
                tr.inc("dve", v.drain(), tag=tag)

            def ln_rstd(v, buf, tag, act_form):
                v.reciprocal(buf["rstd"][:], buf["sq"][:])
                v.drain()
                if act_form:
                    v.tensor_scalar(buf["nmu"][:], buf["bn2"][:, 0:1],
                                    buf["rstd"][:, 0:1], -1.0,
                                    ALU.mult, ALU.mult)
                else:
                    v.tensor_scalar(buf["nmu"][:], buf["bn2"][:, 0:1],
                                    -1.0, None, ALU.mult)
                tr.inc("dve", v.drain(), tag=tag)

            def e_col(h):
                return 1024 * (h // 8) + 512 * (h % 2) + 128 * ((h % 8) // 2)

            def rope(eng, src, m1, dst, cos_b, sin_b):
                swap = src[:].rearrange("p (h t f) -> p h t f",
                                        h=HEADS, t=2, f=32)[:, :, ::-1, :]
                eng.tensor_tensor(
                    m1[:].rearrange("p (h f) -> p h f", h=HEADS),
                    src[:].rearrange("p (h f) -> p h f", h=HEADS),
                    cos_b, ALU.mult)
                eng.tensor_tensor(
                    dst[:].rearrange("p (h t f) -> p h t f", t=2, f=32),
                    swap,
                    sin_b.rearrange("p h (t f) -> p h t f", t=2),
                    ALU.mult)
                return eng.tensor_tensor(dst[:], dst[:], m1[:], ALU.add)

            esA.enter_context(nc.named_scope("phaseA"))
            xnT = [xnT_s, xnT_s2]
            for it in range(-1, NTILE + 2):
                tp = it + 1 if 0 <= it + 1 < NTILE else -1
                tf = it if 0 <= it < NTILE else -1
                tb = it - 1 if 0 <= it - 1 < NTILE else -1
                te = it - 2 if 0 <= it - 2 < NTILE else -1

                @block.sync
                def _(sync, tf=tf):
                    if tf >= 0 and tf + 3 < NTILE:
                        tr.wait(sync, "sync", f"x1add_{tf - 1}")
                        s2, i2 = divmod(tf + 3, NCHUNK)
                        sync.dma_start(out=xbuf[(tf + 3) % 4][:],
                                       in_=x_d[s2, 128 * i2:128 * (i2 + 1), :]
                                       ).then_inc(tr.sems["dx"], 16)
                        tr.cnt["dx"] += 16
                        tr.mark(f"x_{tf + 3}", "dx")

                if te >= 0:
                    @block.scalar
                    def _(a, te=te):
                        tr.wait(a, "act", f"lostats_{te}")
                        a.activation(lo["sq"][:], lo["v"][:], AF.Sqrt)
                        tr.inc("act", a.drain(), tag=f"losqrt_{te}")

                @block.vector
                def _(v, tp=tp, tb=tb, te=te):
                    if tb >= 0 and tb % NCHUNK == 0:
                        tr.wait(v, "dve", f"supd_{tb - 1}")
                        tr.wait(v, "dve", f"qsav_{tb - 1}")
                        ins = v.memset(ssb_s[:], 0.0)
                        tr.inc("dve", ins, tag=f"ssb_{tb}")
                    if te >= 0:
                        tr.wait(v, "dve", f"womm_{te}")
                        tr.wait(v, "dve", "bcast_evac")
                        v.tensor_tensor(y_s[:, 0:512], P3[:],
                                        b1row_bc[:, 0:512], ALU.add)
                        ins = v.tensor_tensor(y_s[:, 512:1024], P4[:],
                                              b1row_bc[:, 512:1024], ALU.add)
                        tr.inc("dve", ins, tag=f"yattn_{te}")
                        ln_stats(v, lo, y_s[:], f"lostats_{te}", EPS)
                    if tb >= 0:
                        tr.wait(v, "dve", f"eT01_{tb}")
                        tr.wait(v, "dve", f"qsav_{tb - 1}")
                        v.tensor_tensor(e_s[:, 0:512], P3[:], mask_sb[:],
                                        ALU.mult)
                        ins = v.tensor_tensor(e_s[:, 512:1024], P4[:],
                                              mask_sb[:], ALU.mult)
                        tr.inc("dve", ins, tag=f"mask0_{tb}")
                        tr.wait(v, "dve", f"eT23_{tb}")
                        v.tensor_tensor(e_s[:, 1024:1536], P3[:], mask_sb[:],
                                        ALU.mult)
                        ins = v.tensor_tensor(e_s[:, 1536:2048], P4[:],
                                              mask_sb[:], ALU.mult)
                        tr.inc("dve", ins, tag=f"mask1_{tb}")
                    if te >= 0:
                        tr.wait(v, "dve", f"losqrt_{te}")
                        ln_rstd(v, lo, f"lorstd_{te}", act_form=False)
                        tr.wait(v, "dve", f"x1add_{te - 1}")
                        v.tensor_scalar(t_s[:], y_s[:], lo["nmu"][:, 0:1],
                                        lo["rstd"][:, 0:1], ALU.add, ALU.mult)
                        tr.inc("dve", v.drain(), tag=f"normts_{te}")
                    if tp >= 0:
                        tr.wait(v, "dve", f"x_{tp}")
                        ln_stats(v, l1, xbuf[tp % 4][:], f"l1stats_{tp}", EPS)
                    if tb >= 0:
                        tr.wait(v, "dve", f"tanh_{tb}")
                        tr.wait(v, "dve", f"qsav_{tb}")
                        tr.wait(v, "dve", f"zppT_t_{tb - 1}")
                        v.scalar_tensor_tensor(z_s[:, 0:512], gate_s[:, 0:512],
                                               1.0, P5[:], ALU.add, ALU.mult)
                        ins = v.scalar_tensor_tensor(z_s[:, 512:1024],
                                                     gate_s[:, 512:1024],
                                                     1.0, P6[:], ALU.add,
                                                     ALU.mult)
                        tr.inc("dve", ins, tag=f"z1_{tb}")
                        ln_stats(v, li, z_s[:], f"listats_{tb}", 4.0 * EPS)
                        tr.wait(v, "dve", f"lisqrt_{tb}")
                        ln_rstd(v, li, f"lirstd_{tb}", act_form=False)
                        ins = v.tensor_scalar(z_s[:], z_s[:],
                                              li["nmu"][:, 0:1],
                                              li["rstd"][:, 0:1],
                                              ALU.add, ALU.mult)
                        tr.inc("dve", ins, tag=f"zpp_{tb}")
                    if tp >= 0:
                        tr.wait(v, "dve", f"l1sqrt_{tp}")
                        ln_rstd(v, l1, f"l1rstd_{tp}", act_form=False)
                        tr.wait(v, "dve", "bcast_mm")
                        tr.wait(v, "dve", f"xnT_t_{tp - 1}")
                        ins = v.tensor_scalar(xn_s[:], xbuf[tp % 4][:],
                                              l1["nmu"][:, 0:1],
                                              l1["rstd"][:, 0:1],
                                              ALU.add, ALU.mult)
                        tr.inc("dve", ins, tag=f"xn_{tp}")

                @block.scalar
                def _(a, tf=tf, tb=tb):
                    if tf >= 0:
                        tr.wait(a, "act", f"qmm_{tf}")
                        tr.wait(a, "act", f"ropeq_{tf - 1}")
                        a.activation(q_tm[:, 0:512], P1[:], AF.Silu)
                        ins = a.activation(q_tm[:, 512:1024], P2[:], AF.Silu)
                        tr.inc("act", ins, tag=f"qsilu_{tf}")
                    if tb >= 0:
                        tr.wait(a, "act", f"g2mm_{tb}")
                        tr.wait(a, "act", f"z1_{tb - 1}")
                        a.activation(gate_s[:, 0:512], P3[:], AF.Tanh,
                                     scale=0.5)
                        ins = a.activation(gate_s[:, 512:1024], P4[:],
                                           AF.Tanh, scale=0.5)
                        tr.inc("act", ins, tag=f"tanh_{tb}")
                    if tf >= 0:
                        tr.wait(a, "act", f"kmm_{tf}")
                        tr.wait(a, "act", f"ropek_{tf - 1}")
                        a.activation(k_tm[:, 0:512], P1[:], AF.Silu)
                        ins = a.activation(k_tm[:, 512:1024], P2[:], AF.Silu)
                        tr.inc("act", ins, tag=f"ksilu_{tf}")

                @block.scalar
                def _(a, tp=tp, tf=tf, tb=tb):
                    if tb >= 0:
                        tr.wait(a, "act", f"listats_{tb}")
                        a.activation(li["sq"][:], li["v"][:], AF.Sqrt)
                        tr.inc("act", a.drain(), tag=f"lisqrt_{tb}")
                    if tp >= 0:
                        tr.wait(a, "act", f"l1stats_{tp}")
                        a.activation(l1["sq"][:], l1["v"][:], AF.Sqrt)
                        tr.inc("act", a.drain(), tag=f"l1sqrt_{tp}")
                    if tf >= 0:
                        tr.wait(a, "act", f"qt_t_{tf}")
                        tr.wait(a, "act", f"qsav_{tf - 1}")
                        ins = a.copy(qt_s[:], PT[:])
                        tr.inc("act", ins, tag=f"qt_e_{tf}")
                    if tb >= 0:
                        tr.wait(a, "act", f"zppT_t_{tb}")
                        tr.wait(a, "act", f"womm_{tb - 1}")
                        ins = a.copy(zppT_s[:], PT[:])
                        tr.inc("act", ins, tag=f"zppT_e_{tb}")
                    if tf >= 0:
                        tr.wait(a, "act", f"vmm_{tf}")
                        tr.wait(a, "act", f"supd_{tf - 1}")
                        a.copy(v_s[:, 0:512], P1[:])
                        ins = a.copy(v_s[:, 512:1024], P2[:])
                        tr.inc("act", ins, tag=f"vevac_{tf}")
                        tr.wait(a, "act", f"kt_t_{tf}")
                        tr.wait(a, "act", f"eT23_{tf - 1}")
                        ins = a.copy(kt_s[:], PT[:])
                        tr.inc("act", ins, tag=f"kt_e_{tf}")
                    if tp >= 0:
                        tr.wait(a, "act", f"xnT_t_{tp}")
                        tr.wait(a, "act", f"g1mm_{tp - 2}")
                        ins = a.copy(xnT[tp % 2][:], PT[:])
                        tr.inc("act", ins, tag=f"xnT_e_{tp}")
                    if tf >= 1 and tf % NCHUNK != 0:
                        tr.wait(a, "act", f"supd_{tf - 1}")
                        tr.wait(a, "act", f"qsav_{tf - 1}")
                        ins = a.copy(ssb_s[:], P0[:])
                        tr.inc("act", ins, tag=f"ssb_{tf}")
                    if tf >= 0:
                        tr.wait(a, "act", f"g1mm_{tf}")
                        tr.wait(a, "act", f"g2mm_{tf - 1}")
                        ins = a.copy(g1t_s[:], P5[0:64, 0:128])
                        tr.inc("act", ins, tag=f"g1evac_{tf}")

                @block.tensor
                def _(pe, tp=tp, tf=tf, tb=tb):
                    if tb >= 0:
                        ib = tb % NCHUNK
                        tr.wait(pe, "pe", f"kt_e_{tb}")
                        tr.wait(pe, "pe", f"ksilu_{tb}")
                        tr.wait(pe, "pe", "bcast_evac")
                        tr.wait(pe, "pe", f"yattn_{tb - 1}")
                        for h in range(8):
                            base = 64 * (h % 2)
                            bank = P3 if h % 2 == 0 else P4
                            col = 128 * ((h % 8) // 2)
                            ins = pe.matmul(
                                bank[:, col:col + 128],
                                kt_s[base:base + 64,
                                     128 * (h // 2):128 * (h // 2) + 128],
                                qt_s[base:base + 64,
                                     128 * (h // 2):128 * (h // 2) + 128],
                                start=(h < 2), stop=(h >= 6),
                                tile_position=(base, 0))
                        tr.inc("pe", ins, tag=f"eT01_{tb}")
                    if tf >= 0:
                        tr.wait(pe, "pe", f"xnT_e_{tf}")
                        tr.wait(pe, "pe", "w_q")
                        tr.wait(pe, "pe", "b1row_evac")
                        tr.wait(pe, "pe", f"vevac_{tf - 1}")
                        for c in range(8):
                            lhs = xnT[tf % 2][:, 128 * c:128 * (c + 1)]
                            pe.matmul(P1[:], lhs, wq_sb[:, D * c:D * c + 512],
                                      start=(c == 0), stop=(c == 7))
                            ins = pe.matmul(P2[:], lhs,
                                            wq_sb[:, D * c + 512:D * c + 1024],
                                            start=(c == 0), stop=(c == 7))
                        tr.inc("pe", ins, tag=f"qmm_{tf}")
                    if tb >= 0:
                        tr.wait(pe, "pe", f"mask0_{tb}")
                        for hh in range(8):
                            h = 8 + hh
                            base = 64 * (h % 2)
                            bank = P3 if h % 2 == 0 else P4
                            col = 128 * ((h % 8) // 2)
                            ins = pe.matmul(
                                bank[:, col:col + 128],
                                kt_s[base:base + 64,
                                     128 * (h // 2):128 * (h // 2) + 128],
                                qt_s[base:base + 64,
                                     128 * (h // 2):128 * (h // 2) + 128],
                                start=(hh < 2), stop=(hh >= 6),
                                tile_position=(base, 0))
                        tr.inc("pe", ins, tag=f"eT23_{tb}")
                        tr.wait(pe, "pe", f"g1evac_{tb}")
                        tr.wait(pe, "pe", "w_g2")
                        tr.wait(pe, "pe", f"mask1_{tb}")
                        pe.matmul(P3[:], g1t_s[:], wg2_sb[:, 0:512],
                                  start=True, stop=True)
                        ins = pe.matmul(P4[:], g1t_s[:], wg2_sb[:, 512:1024],
                                        start=True, stop=True)
                        tr.inc("pe", ins, tag=f"g2mm_{tb}")
                        tr.wait(pe, "pe", f"ssb_{tb}")
                        tr.wait(pe, "pe", f"mask1_{tb}")
                        tr.wait(pe, "pe", f"vevac_{tb}")
                        tr.wait(pe, "pe", f"g1evac_{tb}")
                        tr.wait(pe, "pe", f"z1_{tb - 1}")
                        for h in range(HEADS):
                            base = 64 * (h % 2)
                            bank = P5 if h % 2 == 0 else P6
                            col = 64 * (h // 2)
                            pe.matmul(bank[:, col:col + 64],
                                      qt_s[base:base + 64,
                                           128 * (h // 2):128 * (h // 2) + 128],
                                      ssb_s[base:base + 64, col:col + 64],
                                      start=(h < 2), stop=False,
                                      tile_position=(base, 0))
                        for h in range(HEADS):
                            bank = P5 if h % 2 == 0 else P6
                            col = 64 * (h // 2)
                            ins = pe.matmul(bank[:, col:col + 64],
                                            e_s[:, e_col(h):e_col(h) + 128],
                                            v_s[:, 64 * h:64 * h + 64],
                                            start=False, stop=(h >= 14))
                        tr.inc("pe", ins, tag=f"qsav_{tb}")
                        for h in range(HEADS):
                            base = 64 * (h % 2)
                            col = 64 * (h // 2)
                            ins = pe.matmul(P0[base:base + 64, col:col + 64],
                                            kr_s[:, 64 * h:64 * h + 64],
                                            v_s[:, 64 * h:64 * h + 64],
                                            start=(ib == 0 and h <= 1),
                                            stop=False,
                                            tile_position=(0, base))
                        tr.inc("pe", ins, tag=f"supd_{tb}")
                    if tf >= 0:
                        tr.wait(pe, "pe", "w_k")
                        tr.wait(pe, "pe", f"qsilu_{tf}")
                        for c in range(8):
                            lhs = xnT[tf % 2][:, 128 * c:128 * (c + 1)]
                            pe.matmul(P1[:], lhs, wk_sb[:, D * c:D * c + 512],
                                      start=(c == 0), stop=(c == 7))
                            ins = pe.matmul(P2[:], lhs,
                                            wk_sb[:, D * c + 512:D * c + 1024],
                                            start=(c == 0), stop=(c == 7))
                        tr.inc("pe", ins, tag=f"kmm_{tf}")
                        tr.wait(pe, "pe", "w_v")
                        tr.wait(pe, "pe", f"ksilu_{tf}")
                        for c in range(8):
                            lhs = xnT[tf % 2][:, 128 * c:128 * (c + 1)]
                            pe.matmul(P1[:], lhs, wv_sb[:, D * c:D * c + 512],
                                      start=(c == 0), stop=(c == 7))
                            ins = pe.matmul(P2[:], lhs,
                                            wv_sb[:, D * c + 512:D * c + 1024],
                                            start=(c == 0), stop=(c == 7))
                        tr.inc("pe", ins, tag=f"vmm_{tf}")
                        tr.wait(pe, "pe", "w_g1")
                        if tb >= 0:
                            tr.wait(pe, "pe", f"z1_{tb}")
                        for c in range(8):
                            ins = pe.matmul(P5[0:64, 0:128],
                                            wg1_sb[:, HD * c:HD * (c + 1)],
                                            xnT[tf % 2][:, 128 * c:128 * (c + 1)],
                                            start=(c == 0), stop=(c == 7))
                        tr.inc("pe", ins, tag=f"g1mm_{tf}")
                        tr.wait(pe, "pe", f"ropeq_{tf}")
                        tr.wait(pe, "pe", f"xnT_e_{tf}")
                        for c in range(8):
                            ins = pe.transpose(PT[:, 128 * c:128 * (c + 1)],
                                               qr_s[:, 128 * c:128 * (c + 1)],
                                               ident_s[:])
                        tr.inc("pe", ins, tag=f"qt_t_{tf}")
                    if tb >= 0:
                        tr.wait(pe, "pe", f"zpp_{tb}")
                        tr.wait(pe, "pe", f"qt_e_{tf}" if tf >= 0
                                else f"kt_e_{tb}")
                        for c in range(8):
                            ins = pe.transpose(PT[:, 128 * c:128 * (c + 1)],
                                               z_s[:, 128 * c:128 * (c + 1)],
                                               ident_s[:])
                        tr.inc("pe", ins, tag=f"zppT_t_{tb}")
                    if tf >= 0:
                        tr.wait(pe, "pe", f"ropek_{tf}")
                        tr.wait(pe, "pe", f"zppT_e_{tb}" if tb >= 0
                                else f"qt_e_{tf}")
                        for c in range(8):
                            ins = pe.transpose(PT[:, 128 * c:128 * (c + 1)],
                                               kr_s[:, 128 * c:128 * (c + 1)],
                                               ident_s[:])
                        tr.inc("pe", ins, tag=f"kt_t_{tf}")
                    if tp >= 0:
                        tr.wait(pe, "pe", f"xn_{tp}")
                        tr.wait(pe, "pe", f"kt_e_{tf}" if tf >= 0
                                else "bcast_evac")
                        for c in range(8):
                            ins = pe.transpose(PT[:, 128 * c:128 * (c + 1)],
                                               xn_s[:, 128 * c:128 * (c + 1)],
                                               ident_s[:])
                        tr.inc("pe", ins, tag=f"xnT_t_{tp}")
                    if tb >= 0:
                        tr.wait(pe, "pe", f"zppT_e_{tb}")
                        tr.wait(pe, "pe", f"tanh_{tb}")
                        tr.wait(pe, "pe", "woprime")
                        for c in range(8):
                            lhs = zppT_s[:, 128 * c:128 * (c + 1)]
                            pe.matmul(P3[:], lhs, wo_sb[:, D * c:D * c + 512],
                                      start=(c == 0), stop=(c == 7))
                            ins = pe.matmul(P4[:], lhs,
                                            wo_sb[:, D * c + 512:D * c + 1024],
                                            start=(c == 0), stop=(c == 7))
                        tr.inc("pe", ins, tag=f"womm_{tb}")

                @block.gpsimd
                def _(gp, tf=tf, te=te):
                    if tf >= 0:
                        cos_sl2 = cos_sb[:, HD * (tf % NCHUNK):
                                         HD * (tf % NCHUNK + 1)]
                        sin_sl2 = sin_sb[:, HD * (tf % NCHUNK):
                                         HD * (tf % NCHUNK + 1)]
                        cb = cos_sl2.unsqueeze(1).broadcast_to((128, HEADS, HD))
                        sb_ = sin_sl2.unsqueeze(1).broadcast_to((128, HEADS, HD))
                        tr.wait(gp, "pool", f"qsilu_{tf}")
                        tr.wait(gp, "pool", f"qt_t_{tf - 1}")
                        ins = rope(gp, q_tm, qm1, qr_s, cb, sb_)
                        tr.inc("pool", ins, tag=f"ropeq_{tf}")
                        tr.wait(gp, "pool", f"ksilu_{tf}")
                        tr.wait(gp, "pool", f"supd_{tf - 1}")
                        ins = rope(gp, k_tm, km1, kr_s, cb, sb_)
                        tr.inc("pool", ins, tag=f"ropek_{tf}")
                    if te >= 0:
                        tr.wait(gp, "pool", f"normts_{te}")
                        tr.wait(gp, "pool", f"x1dma_{te - 1}")
                        ins = gp.tensor_tensor(x1_s[:], xbuf[te % 4][:],
                                               t_s[:], ALU.add)
                        tr.inc("pool", ins, tag=f"x1add_{te}")
                        gp.dma_start(out=x1_d[te], in_=x1_s[:]
                                     ).then_inc(tr.sems["dxo"], 16)
                        tr.cnt["dxo"] += 16
                        tr.mark(f"x1dma_{te}", "dxo")
                        if te % 2 == 0 and te // 2 < 8:
                            c = te // 2
                            gp.dma_start(
                                out=w1_sb[:, WIDTH * c:WIDTH * (c + 1)],
                                in_=w1_d[128 * c:128 * (c + 1), :]
                                ).then_inc(tr.sems["dw"], 16)
                            tr.cnt["dw"] += 16
                            if c == 7:
                                tr.mark("w_w1", "dw")

        esB = contextlib.ExitStack()
        with esB:
            sbB = lambda name, shape, dt=BF16: esB.enter_context(
                nc.sbuf_tensor(name, shape, dt))
            w2_sb = sbB("w2_sb", [128, 32 * D])
            x1b = [sbB(f"x1b{k}", [128, D]) for k in range(8)]
            xn2_s = sbB("xn2_s", [128, D])
            xn2T_s = sbB("xn2T_s", [128, 8 * 512])
            hT_s = sbB("hT_s", [128, 32 * 512])
            y2_s = sbB("y2_s", [128, D], F32)
            x2_s = [sbB("x2_0", [128, D], F32)]
            lnbufB = {}
            for pfx in ("l2", "lf"):
                lnbufB[pfx] = dict(
                    bna=sbB(f"{pfx}_bna", [128, 12], F32),
                    bn2=sbB(f"{pfx}_bn2", [128, 2], F32),
                    v=sbB(f"{pfx}_v", [128, 1], F32),
                    sq=sbB(f"{pfx}_sq", [128, 1], F32),
                    rstd=sbB(f"{pfx}_rstd", [128, 1], F32),
                    nmu=sbB(f"{pfx}_nmu", [128, 1], F32),
                )
            l2, lf = lnbufB["l2"], lnbufB["lf"]

            def ln_statsB(v, buf, src_ap, tag, eps):
                v.bn_stats(buf["bna"][:, 0:6], src_ap[:, 0:512])
                v.bn_stats(buf["bna"][:, 6:12], src_ap[:, 512:1024])
                v.drain()
                v.bn_aggr(buf["bn2"][:], buf["bna"][:])
                v.drain()
                v.tensor_scalar(buf["v"][:], buf["bn2"][:, 1:2],
                                float(eps), None, ALU.add)
                tr.inc("dve", v.drain(), tag=tag)

            @block.gpsimd
            def _(gp):
                tr.wait(gp, "pool", f"womm_{NTILE - 1}")
                tr.wait(gp, "pool", f"normts_{NTILE - 1}")
                tr.wait(gp, "pool", f"zppT_e_{NTILE - 1}")
                tr.wait(gp, "pool", f"x1add_{NTILE - 1}")
                for c in range(32):
                    gp.dma_start(out=w2_sb[:, D * c:D * (c + 1)],
                                 in_=w2_d[128 * c:128 * (c + 1), :]
                                 ).then_inc(tr.sems["dw2"], 16)
                    tr.cnt["dw2"] += 16
                tr.mark("w_w2", "dw2")

            @block.sync
            def _(sync):
                tr.wait(sync, "sync", f"womm_{NTILE - 1}")
                tr.wait(sync, "sync", f"normts_{NTILE - 1}")
                tr.wait(sync, "sync", f"zppT_e_{NTILE - 1}")
                tr.wait(sync, "sync", f"x1add_{NTILE - 1}")
                tr.wait(sync, "sync", f"x1dma_{NTILE - 1}")
                for t in range(8):
                    tr.wait(sync, "sync", f"x1dma_{t}")
                    sync.dma_start(out=x1b[t][:], in_=x1_d[t]
                                   ).then_inc(tr.sems["dx1"], 16)
                    tr.cnt["dx1"] += 16
                    tr.mark(f"x1in_{t}", "dx1")

            xn2T_s2 = sbB("xn2T_s2", [128, 8 * 512])
            xn2T = [xn2T_s, xn2T_s2]

            esB.enter_context(nc.named_scope("phaseB"))
            NG = NTILE // 4
            banks = [P1, P2, P3, P4]
            for g in range(-1, NG + 1):
                gf = g if 0 <= g < NG else -1
                gp = g + 1 if 0 <= g + 1 < NG else -1
                gb = g - 1 if g - 1 >= 0 else -1

                @block.sync
                def _(sync, gf=gf, gp=gp, gb=gb):
                    if gb >= 0:
                        for j in range(4):
                            t = 4 * gb + j
                            tr.wait(sync, "sync", f"x2add_{t}")
                            s2, i2 = divmod(t, NCHUNK)
                            sync.dma_start(
                                out=out_d[s2, 128 * i2:128 * (i2 + 1), :],
                                in_=x2_s[0][:]
                                ).then_inc(tr.sems["dout"], 16)
                            tr.cnt["dout"] += 16
                            tr.mark(f"x2dma_{t}", "dout")
                    if gp >= 2:
                        for j in range(4):
                            t2 = 4 * gp + j
                            tr.wait(sync, "sync", f"x2add_{t2 - 8}")
                            tr.wait(sync, "sync", f"x1dma_{t2}")
                            sync.dma_start(out=x1b[t2 % 8][:], in_=x1_d[t2]
                                           ).then_inc(tr.sems["dx1"], 16)
                            tr.cnt["dx1"] += 16
                            tr.mark(f"x1in_{t2}", "dx1")

                @block.vector
                def _(v, gf=gf, gp=gp):
                    if gp >= 0:
                        for j in range(4):
                            t = 4 * gp + j
                            tr.wait(v, "dve", f"x1in_{t}")
                            tr.wait(v, "dve", f"xn2_{t - 1}")
                            ln_statsB(v, l2, x1b[t % 8][:], f"l2stats_{t}",
                                      EPS)
                            tr.wait(v, "dve", f"l2sqrt_{t}")
                            v.reciprocal(l2["rstd"][:], l2["sq"][:])
                            v.drain()
                            v.tensor_scalar(l2["nmu"][:], l2["bn2"][:, 0:1],
                                            l2["rstd"][:, 0:1], -1.0,
                                            ALU.mult, ALU.mult)
                            tr.inc("dve", v.drain(), tag=f"l2rstd_{t}")
                    if gf >= 0:
                        for j in range(4):
                            t = 4 * gf + j
                            tr.wait(v, "dve", f"ymm_{t}")
                            tr.wait(v, "dve", "bcast_evac")
                            v.tensor_tensor(y2_s[:, 0:512], P5[:],
                                            b2row_bc[:, 0:512], ALU.add)
                            ins = v.tensor_tensor(y2_s[:, 512:1024], P6[:],
                                                  b2row_bc[:, 512:1024],
                                                  ALU.add)
                            tr.inc("dve", ins, tag=f"y2_{t}")
                            ln_statsB(v, lf, y2_s[:], f"lfstats_{t}", EPS)
                            tr.wait(v, "dve", f"lfsqrt_{t}")
                            v.reciprocal(lf["rstd"][:], lf["sq"][:])
                            v.drain()
                            v.tensor_scalar(lf["nmu"][:], lf["bn2"][:, 0:1],
                                            -1.0, None, ALU.mult)
                            v.drain()
                            tr.wait(v, "dve", f"x2dma_{t - 1}")
                            v.tensor_scalar(y2_s[:], y2_s[:],
                                            lf["nmu"][:, 0:1],
                                            lf["rstd"][:, 0:1],
                                            ALU.add, ALU.mult)
                            v.tensor_tensor(x2_s[0][:], y2_s[:],
                                            x1b[t % 8][:], ALU.add)
                            tr.inc("dve", v.drain(), tag=f"x2add_{t}")

                @block.scalar
                def _(a, gf=gf, gp=gp):
                    for j in range(4):
                        if gf >= 0:
                            for c in range(8 * j, 8 * j + 8):
                                tr.wait(a, "act", f"hmm_{gf}_{c}")
                                if gf > 0 and c == 0:
                                    tr.wait(a, "act", f"ymm_{4 * gf - 1}")
                                ins = a.activation(
                                    hT_s[:, 512 * c:512 * (c + 1)],
                                    banks[c % 4][:], AF.Relu,
                                    bias=b1_sb[:, c:c + 1])
                                tr.inc("act", ins, tag=f"relu_{gf}_{c}")
                        if gp >= 0:
                            t = 4 * gp + j
                            tr.wait(a, "act", f"l2stats_{t}")
                            a.activation(l2["sq"][:], l2["v"][:], AF.Sqrt)
                            tr.inc("act", a.drain(), tag=f"l2sqrt_{t}")
                            tr.wait(a, "act", f"l2rstd_{t}")
                            tr.wait(a, "act", f"xn2T_t_{t - 1}")
                            ins = a.activation(xn2_s[:], x1b[t % 8][:],
                                               AF.Identity,
                                               bias=l2["nmu"][:, 0:1],
                                               scale=l2["rstd"][:, 0:1])
                            tr.inc("act", ins, tag=f"xn2_{t}")
                        if gp >= 0:
                            t = 4 * gp + j
                            tr.wait(a, "act", f"xn2T_t_{t}")
                            if gp >= 2:
                                tr.wait(a, "act", f"hmm_{gp - 2}_31")
                            ins = a.copy(
                                xn2T[gp % 2][:].rearrange(
                                    "p (c q) -> p c q", c=8)
                                [:, :, 128 * j:128 * (j + 1)],
                                PT[:].rearrange("p (c q) -> p c q", c=8))
                            tr.inc("act", ins, tag=f"xn2T_e_{t}")
                    if gf >= 0:
                        for j in range(4):
                            t = 4 * gf + j
                            tr.wait(a, "act", f"lfstats_{t}")
                            a.activation(lf["sq"][:], lf["v"][:], AF.Sqrt)
                            tr.inc("act", a.drain(), tag=f"lfsqrt_{t}")

                @block.tensor
                def _(pe, gf=gf, gp=gp):
                    if gf >= 0:
                        tr.wait(pe, "pe", f"xn2T_e_{4 * gf + 3}")
                        tr.wait(pe, "pe", "w_w1")
                        if gf == 0:
                            tr.wait(pe, "pe", f"yattn_{NTILE - 1}")
                            tr.wait(pe, "pe", f"g1evac_{NTILE - 1}")
                            tr.wait(pe, "pe", f"tanh_{NTILE - 1}")
                        for c in range(32):
                            if c >= 4:
                                tr.wait(pe, "pe", f"relu_{gf}_{c - 4}")
                            elif gf > 0:
                                tr.wait(pe, "pe", f"relu_{gf - 1}_{28 + c}")
                            for k in range(8):
                                ins = pe.matmul(
                                    banks[c % 4][:],
                                    w1_sb[:, WIDTH * k + 128 * c:
                                          WIDTH * k + 128 * (c + 1)],
                                    xn2T[gf % 2][:, 512 * k:512 * (k + 1)],
                                    start=(k == 0), stop=(k == 7))
                            tr.inc("pe", ins, tag=f"hmm_{gf}_{c}")
                            if gp >= 0 and c % 8 == 7:
                                j = c // 8
                                t = 4 * gp + j
                                tr.wait(pe, "pe", f"xn2_{t}")
                                tr.wait(pe, "pe", f"xn2T_e_{t - 1}")
                                if t == 0:
                                    tr.wait(pe, "pe", f"zppT_e_{NTILE - 1}")
                                for cc in range(8):
                                    ins = pe.transpose(
                                        PT[:, 128 * cc:128 * (cc + 1)],
                                        xn2_s[:, 128 * cc:128 * (cc + 1)],
                                        ident_s[:])
                                tr.inc("pe", ins, tag=f"xn2T_t_{t}")
                        for j in range(4):
                            t = 4 * gf + j
                            tr.wait(pe, "pe", f"relu_{gf}_31")
                            tr.wait(pe, "pe", "w_w2")
                            if t == 0:
                                tr.wait(pe, "pe", f"z1_{NTILE - 1}")
                            tr.wait(pe, "pe", f"y2_{t - 1}")
                            for c in range(32):
                                lhs = hT_s[:, 512 * c + 128 * j:
                                           512 * c + 128 * (j + 1)]
                                pe.matmul(P5[:], lhs,
                                          w2_sb[:, D * c:D * c + 512],
                                          start=(c == 0), stop=(c == 31))
                                ins = pe.matmul(
                                    P6[:], lhs,
                                    w2_sb[:, D * c + 512:D * c + 1024],
                                    start=(c == 0), stop=(c == 31))
                            tr.inc("pe", ins, tag=f"ymm_{t}")
                    elif gp >= 0:
                        for j in range(4):
                            t = 4 * gp + j
                            tr.wait(pe, "pe", f"xn2_{t}")
                            tr.wait(pe, "pe", f"xn2T_e_{t - 1}")
                            if t == 0:
                                tr.wait(pe, "pe", f"zppT_e_{NTILE - 1}")
                            for cc in range(8):
                                ins = pe.transpose(
                                    PT[:, 128 * cc:128 * (cc + 1)],
                                    xn2_s[:, 128 * cc:128 * (cc + 1)],
                                    ident_s[:])
                            tr.inc("pe", ins, tag=f"xn2T_t_{t}")

            @block.sync
            def _(sync):
                sync.wait_ge(tr.sems["dout"], 16 * NTILE)

    return nc


def build_program(debug_x1=False):
    tr1 = Tracker()
    _emit(tr1, debug_x1)
    tr2 = Tracker(plan=tr1.m)
    return _emit(tr2, debug_x1)




def _host_constants():
    pos = np.arange(SEQ, dtype=np.float64)
    inv = 1.0 / (10000.0 ** (np.arange(0, HD, 2, dtype=np.float64) / HD))
    ang = pos[:, None] * inv[None, :]
    cos = np.concatenate([np.cos(ang), np.cos(ang)], axis=1)
    sin = np.concatenate([np.sin(ang), np.sin(ang)], axis=1)
    sgn = np.ones((1, HD)); sgn[0, :32] = -1.0
    cos_t = cos.astype(ml_dtypes.bfloat16)
    sin_t = (sin * sgn).astype(ml_dtypes.bfloat16)
    mask4 = np.tile(np.triu(np.ones((128, 128), dtype=np.float32)), (1, 4))
    mask4 = mask4.astype(ml_dtypes.bfloat16)
    ident = np.eye(128, dtype=np.float32).astype(ml_dtypes.bfloat16)
    ones = np.ones((1, 128), dtype=np.float32)
    return cos_t, sin_t, mask4, ident, ones


_NC_CACHE = {}


def kernel(L0_attn_x, L0_attn_Wq, L0_attn_Wk, L0_attn_Wv, L0_attn_Wo,
           L0_attn_Wg1, L0_attn_Wg2, L0_attn_ln_w, L0_attn_ln_b,
           L0_ffn_w1, L0_ffn_b1, L0_ffn_w2, L0_ffn_b2):
    x = np.ascontiguousarray(np.asarray(L0_attn_x, dtype=np.float32))
    cos_t, sin_t, mask4, ident, ones = _host_constants()
    if "nc" not in _NC_CACHE:
        _NC_CACHE["nc"] = build_program()
    nc = _NC_CACHE["nc"]
    common = {
        "wq": np.ascontiguousarray(np.asarray(L0_attn_Wq, np.float32)),
        "wk": np.ascontiguousarray(np.asarray(L0_attn_Wk, np.float32)),
        "wv": np.ascontiguousarray(np.asarray(L0_attn_Wv, np.float32)),
        "wo": np.ascontiguousarray(np.asarray(L0_attn_Wo, np.float32)),
        "wg1": np.ascontiguousarray(np.asarray(L0_attn_Wg1, np.float32)),
        "wg2": np.ascontiguousarray(np.asarray(L0_attn_Wg2, np.float32)),
        "lnw": np.ascontiguousarray(np.asarray(L0_attn_ln_w, np.float32)),
        "lnb": np.ascontiguousarray(np.asarray(L0_attn_ln_b, np.float32)),
        "w1": np.ascontiguousarray(np.asarray(L0_ffn_w1, np.float32)),
        "b1": np.ascontiguousarray(np.asarray(L0_ffn_b1, np.float32)),
        "w2": np.ascontiguousarray(np.asarray(L0_ffn_w2, np.float32)),
        "b2": np.ascontiguousarray(np.asarray(L0_ffn_b2, np.float32)),
        "costab": cos_t, "sintab": sin_t, "mask4": mask4,
        "ident": ident, "onesrow": ones,
    }
    in_maps = []
    for c in range(N_CORES):
        m = dict(common)
        m["x"] = np.ascontiguousarray(x[NSEQ * c:NSEQ * (c + 1)])
        in_maps.append(m)
    res = run_bass_kernel_spmd(nc, in_maps, core_ids=list(range(N_CORES)))
    out = np.concatenate([r["out"] for r in res.results], axis=0)
    return out.reshape(B, SEQ, D)


# version 15
# speedup vs baseline: 1.1146x; 1.1146x over previous
import contextlib
import numpy as np
import ml_dtypes
import concourse.bass as bass
import concourse.mybir as mybir
from concourse.bass_utils import run_bass_kernel_spmd
from concourse.alu_op_type import AluOpType as ALU

F32 = mybir.dt.float32
BF16 = mybir.dt.bfloat16
AF = mybir.ActivationFunctionType

N_CORES = 8
B, SEQ, D = 32, 2048, 1024
NSEQ = B // N_CORES
NCHUNK = SEQ // 128
NTILE = NSEQ * NCHUNK
HEADS, HD = 16, 64
WIDTH = 4096
EPS = 1e-5


class Tracker:

    def __init__(self, plan=None):
        self.sems = {}
        self.cnt = {}
        self.m = {}
        self.last_wait = {}
        self.plan = plan

    def add_sem(self, name, handle):
        self.sems[name] = handle
        self.cnt[name] = 0

    def inc(self, sem_name, instr, n=1, tag=None):
        instr.then_inc(self.sems[sem_name], n)
        self.cnt[sem_name] += n
        if tag is not None:
            self.m[tag] = (sem_name, self.cnt[sem_name])
            if self.plan is not None:
                assert self.plan.get(tag) == (sem_name, self.cnt[sem_name]), \
                    f"plan divergence at {tag}"
        return self.cnt[sem_name]

    def mark(self, tag, sem_name):
        self.m[tag] = (sem_name, self.cnt[sem_name])
        if self.plan is not None:
            assert self.plan.get(tag) == (sem_name, self.cnt[sem_name]), \
                f"plan divergence at {tag}"

    def wait(self, eng, eng_key, tag):
        src = self.plan if self.plan is not None else self.m
        if tag not in src:
            return
        sem_name, count = src[tag]
        if count <= 0:
            return
        key = (eng_key, sem_name)
        if self.last_wait.get(key, -1) >= count:
            return
        eng.wait_ge(self.sems[sem_name], count)
        self.last_wait[key] = count


def _emit(tr, debug_x1=False):
    nc = bass.Bass("TRN2", target_bir_lowering=False)

    x_d = nc.dram_tensor("x", [NSEQ, SEQ, D], F32, kind="ExternalInput").ap()
    wq_d = nc.dram_tensor("wq", [D, D], F32, kind="ExternalInput").ap()
    wk_d = nc.dram_tensor("wk", [D, D], F32, kind="ExternalInput").ap()
    wv_d = nc.dram_tensor("wv", [D, D], F32, kind="ExternalInput").ap()
    wo_d = nc.dram_tensor("wo", [D, D], F32, kind="ExternalInput").ap()
    wg1_d = nc.dram_tensor("wg1", [D, HD], F32, kind="ExternalInput").ap()
    wg2_d = nc.dram_tensor("wg2", [HD, D], F32, kind="ExternalInput").ap()
    lnw_d = nc.dram_tensor("lnw", [D], F32, kind="ExternalInput").ap()
    lnb_d = nc.dram_tensor("lnb", [D], F32, kind="ExternalInput").ap()
    w1_d = nc.dram_tensor("w1", [D, WIDTH], F32, kind="ExternalInput").ap()
    b1_d = nc.dram_tensor("b1", [WIDTH], F32, kind="ExternalInput").ap()
    w2_d = nc.dram_tensor("w2", [WIDTH, D], F32, kind="ExternalInput").ap()
    b2_d = nc.dram_tensor("b2", [D], F32, kind="ExternalInput").ap()
    cos_d = nc.dram_tensor("costab", [SEQ, HD], BF16, kind="ExternalInput").ap()
    sin_d = nc.dram_tensor("sintab", [SEQ, HD], BF16, kind="ExternalInput").ap()
    mask_d = nc.dram_tensor("mask4", [128, 512], BF16, kind="ExternalInput").ap()
    ident_d = nc.dram_tensor("ident", [128, 128], BF16, kind="ExternalInput").ap()
    ones_d = nc.dram_tensor("onesrow", [1, 128], F32, kind="ExternalInput").ap()
    out_d = nc.dram_tensor("out", [NSEQ, SEQ, D], F32, kind="ExternalOutput").ap()
    x1_kind = "ExternalOutput" if debug_x1 else "Internal"
    x1_d = nc.dram_tensor("x1scr", [NTILE, 128, D], BF16, kind=x1_kind).ap()

    es = contextlib.ExitStack()
    with es:
        for s in ["pe", "dve", "act", "pool", "dx", "dw", "dc", "dxo",
                  "dx1", "dout", "dw2"]:
            tr.add_sem(s, es.enter_context(nc.semaphore(s)))

        w1_sb = es.enter_context(nc.sbuf_tensor("w1_sb", [128, 8 * WIDTH], BF16))
        b1_sb = es.enter_context(nc.sbuf_tensor("b1_sb", [128, 32], F32))
        b2row_bc = es.enter_context(nc.sbuf_tensor("b2row_bc", [128, D], F32))
        ident_s = es.enter_context(nc.sbuf_tensor("ident_s", [128, 128], BF16))

        P0 = es.enter_context(nc.psum_tensor("P0", [128, 512], F32))
        P1 = es.enter_context(nc.psum_tensor("P1", [128, 512], F32))
        P2 = es.enter_context(nc.psum_tensor("P2", [128, 512], F32))
        P3 = es.enter_context(nc.psum_tensor("P3", [128, 512], F32))
        P4 = es.enter_context(nc.psum_tensor("P4", [128, 512], F32))
        P5 = es.enter_context(nc.psum_tensor("P5", [128, 512], F32))
        P6 = es.enter_context(nc.psum_tensor("P6", [128, 512], F32))
        PT = es.enter_context(nc.psum_tensor("PT", [128, 1024], BF16))

        block = es.enter_context(nc.Block())

        esA = contextlib.ExitStack()
        with esA:
            sb = lambda name, shape, dt=BF16: esA.enter_context(
                nc.sbuf_tensor(name, shape, dt))
            wq_sb = sb("wq_sb", [128, 8 * D])
            wk_sb = sb("wk_sb", [128, 8 * D])
            wv_sb = sb("wv_sb", [128, 8 * D])
            wo_sb = sb("wo_sb", [128, 8 * D])
            wg1_sb = sb("wg1_sb", [128, 8 * HD])
            wg2_sb = sb("wg2_sb", [64, D])
            cos_sb = sb("cos_sb", [128, NCHUNK * HD])
            sin_sb = sb("sin_sb", [128, NCHUNK * HD])
            mask_sb = sb("mask_sb", [128, 512])
            b1row_bc = sb("b1row_bc", [128, D], F32)

            esS = contextlib.ExitStack()
            sbS = lambda name, shape, dt=BF16: esS.enter_context(
                nc.sbuf_tensor(name, shape, dt))
            lnw_sb = sbS("lnw_sb", [128, 8], F32)
            lnb_sb = sbS("lnb_sb", [128, 8], F32)
            lnb_bf = sbS("lnb_bf", [128, 8])
            b1row_f = sbS("b1row_f", [1, D], F32)
            b2_row = sbS("b2_row", [1, D], F32)
            ones_s = sbS("ones_s", [1, 128], F32)

            @block.sync
            def _(sync):
                for src, dst in ((ident_d, ident_s), (mask_d, mask_sb)):
                    sync.dma_start(out=dst[:], in_=src).then_inc(tr.sems["dc"], 16)
                    tr.cnt["dc"] += 16
                sync.dma_start(out=ones_s[:], in_=ones_d).then_inc(tr.sems["dc"], 16)
                tr.cnt["dc"] += 16
                for src, dst in ((cos_d, cos_sb), (sin_d, sin_sb)):
                    sync.dma_start(
                        out=dst[:].rearrange("p (i d) -> p i d", i=NCHUNK),
                        in_=src.rearrange("(i p) d -> p i d", p=128),
                    ).then_inc(tr.sems["dc"], 16)
                    tr.cnt["dc"] += 16
                for src, dst in ((lnw_d, lnw_sb), (lnb_d, lnb_sb)):
                    for j2 in range(2):
                        for ho in range(2):
                            for j1 in range(4):
                                off = j1 * 256 + j2 * 128 + ho * 64
                                sync.dma_start(
                                    out=dst[64 * j2:64 * (j2 + 1),
                                            ho * 4 + j1:ho * 4 + j1 + 1],
                                    in_=src[off:off + 64].unsqueeze(1),
                                ).then_inc(tr.sems["dc"], 16)
                                tr.cnt["dc"] += 16
                sync.dma_start(out=b1_sb[:],
                               in_=b1_d.rearrange("(c p) -> p c", p=128),
                               allow_slow_non_contiguous=True
                               ).then_inc(tr.sems["dc"], 16)
                tr.cnt["dc"] += 16
                sync.dma_start(out=b2_row[:], in_=b2_d.unsqueeze(0)
                               ).then_inc(tr.sems["dc"], 16)
                tr.cnt["dc"] += 16
                tr.mark("consts", "dc")

            @block.gpsimd
            def _(gp):
                for wsrc, wdst, tag in ((wq_d, wq_sb, "w_q"), (wk_d, wk_sb, "w_k"),
                                        (wv_d, wv_sb, "w_v")):
                    for c in range(8):
                        gp.dma_start(out=wdst[:, D * c:D * (c + 1)],
                                     in_=wsrc[128 * c:128 * (c + 1), :]
                                     ).then_inc(tr.sems["dw"], 16)
                        tr.cnt["dw"] += 16
                    tr.mark(tag, "dw")
                for c in range(8):
                    ho, jj = divmod(c, 4)
                    for half in range(2):
                        h_old = 2 * (2 * jj + half) + ho
                        gp.dma_start(
                            out=wo_sb[64 * half:64 * (half + 1), D * c:D * (c + 1)],
                            in_=wo_d[64 * h_old:64 * (h_old + 1), :],
                        ).then_inc(tr.sems["dw"], 16)
                        tr.cnt["dw"] += 16
                tr.mark("w_o", "dw")
                gp.dma_start(out=wg1_sb[:].rearrange("p (c f) -> p c f", c=8),
                             in_=wg1_d.rearrange("(c p) f -> p c f", p=128)
                             ).then_inc(tr.sems["dw"], 16)
                tr.cnt["dw"] += 16
                tr.mark("w_g1", "dw")
                wg2_perm = wg2_d.rearrange("k (j ho d) -> k ho j d",
                                           j=8, ho=2, d=64)
                for ho in range(2):
                    gp.dma_start(
                        out=wg2_sb[:, 512 * ho:512 * (ho + 1)].rearrange(
                            "k (j d) -> k j d", j=8),
                        in_=wg2_perm[:, ho],
                    ).then_inc(tr.sems["dw"], 16)
                    tr.cnt["dw"] += 16
                tr.mark("w_g2", "dw")

            @block.vector
            def _(v):
                tr.wait(v, "dve", "consts")
                v.tensor_copy(lnb_bf[:], lnb_sb[:])
                tr.inc("dve", v.drain(), tag="lnb_bf")

            @block.tensor
            def _(pe):
                tr.wait(pe, "pe", "lnb_bf")
                tr.wait(pe, "pe", "w_o")
                for c in range(8):
                    for half in range(2):
                        ins = pe.matmul(
                            (P1 if half == 0 else P2)[0:1, :],
                            lnb_bf[:, c:c + 1],
                            wo_sb[:, D * c + 512 * half:D * c + 512 * (half + 1)],
                            start=(c == 0), stop=(c == 7))
                tr.inc("pe", ins, tag="b1row_mm")

            @block.scalar
            def _(a):
                tr.wait(a, "act", "b1row_mm")
                a.copy(b1row_f[0:1, 0:512], P1[0:1, :])
                a.copy(b1row_f[0:1, 512:1024], P2[0:1, :])
                tr.inc("act", a.drain(), tag="b1row_evac")

            @block.vector
            def _(v):
                tr.wait(v, "dve", "b1row_mm")
                for c in range(8):
                    ins = v.tensor_scalar(wo_sb[:, D * c:D * (c + 1)],
                                          wo_sb[:, D * c:D * (c + 1)],
                                          lnw_sb[:, c:c + 1], None, ALU.mult)
                tr.inc("dve", ins, tag="woprime")

            @block.tensor
            def _(pe):
                tr.wait(pe, "pe", "b1row_evac")
                pe.matmul(P3[:, :], ones_s[:], b1row_f[0:1, 0:512],
                          start=True, stop=True)
                pe.matmul(P4[:, :], ones_s[:], b1row_f[0:1, 512:1024],
                          start=True, stop=True)
                pe.matmul(P5[:, :], ones_s[:], b2_row[0:1, 0:512],
                          start=True, stop=True)
                ins = pe.matmul(P6[:, :], ones_s[:], b2_row[0:1, 512:1024],
                                start=True, stop=True)
                tr.inc("pe", ins, tag="bcast_mm")

            @block.scalar
            def _(a):
                tr.wait(a, "act", "bcast_mm")
                a.copy(b1row_bc[:, 0:512], P3[:, :])
                a.copy(b1row_bc[:, 512:1024], P4[:, :])
                a.copy(b2row_bc[:, 0:512], P5[:, :])
                tr.inc("act", a.copy(b2row_bc[:, 512:1024], P6[:, :]),
                       tag="bcast_evac")

            esS.close()

            xbuf = [sb(f"xA{i}", [128, D], F32) for i in range(4)]
            xn_s = sb("xn_s", [128, D])
            xnT_s = sb("xnT_s", [128, D])
            xnT_s2 = sb("xnT_s2", [128, D])
            q_tm = sb("q_tm", [128, D])
            k_tm = sb("k_tm", [128, D])
            qm1 = sb("qm1", [128, D])
            km1 = sb("km1", [128, D])
            qr_s = sb("qr_s", [128, D])
            kr_s = sb("kr_s", [128, D])
            qt_s = sb("qt_s", [128, D])
            kt_s = sb("kt_s", [128, D])
            v_s = sb("v_s", [128, D])
            g1t_s = sb("g1t_s", [64, 128])
            e_s = sb("e_s", [128, 2048])
            ssb_s = sb("ssb_s", [128, 512])
            gate_s = sb("gate_s", [128, D])
            z_s = sb("z_s", [128, D])
            zppT_s = sb("zppT_s", [128, D])
            y_s = sb("y_s", [128, D], F32)
            t_s = sb("t_s", [128, D], F32)
            x1_s = sb("x1_s", [128, D])
            lnbuf = {}
            for pfx in ("l1", "li", "lo"):
                lnbuf[pfx] = dict(
                    bna=sb(f"{pfx}_bna", [128, 12], F32),
                    bn2=sb(f"{pfx}_bn2", [128, 2], F32),
                    v=sb(f"{pfx}_v", [128, 1], F32),
                    sq=sb(f"{pfx}_sq", [128, 1], F32),
                    rstd=sb(f"{pfx}_rstd", [128, 1], F32),
                    nmu=sb(f"{pfx}_nmu", [128, 1], F32),
                )
            l1, li, lo = lnbuf["l1"], lnbuf["li"], lnbuf["lo"]
            dbg = {}
            if debug_x1:
                for nm, dt in [("dbg_x", F32), ("dbg_xn", BF16)]:
                    dbg[nm] = nc.dram_tensor(nm, [128, D], dt,
                                             kind="ExternalOutput").ap()

            @block.sync
            def _(sync):
                tr.wait(sync, "sync", "bcast_mm")
                tr.wait(sync, "sync", "b1row_evac")
                tr.wait(sync, "sync", "lnb_bf")
                tr.wait(sync, "sync", "woprime")
                for t in range(3):
                    sq, iq = divmod(t, NCHUNK)
                    sync.dma_start(out=xbuf[t % 4][:],
                                   in_=x_d[sq, 128 * iq:128 * (iq + 1), :]
                                   ).then_inc(tr.sems["dx"], 16)
                    tr.cnt["dx"] += 16
                    tr.mark(f"x_{t}", "dx")

            def ln_stats(v, buf, src_ap, tag, eps):
                v.bn_stats(buf["bna"][:, 0:6], src_ap[:, 0:512])
                v.bn_stats(buf["bna"][:, 6:12], src_ap[:, 512:1024])
                v.drain()
                v.bn_aggr(buf["bn2"][:], buf["bna"][:])
                v.drain()
                v.tensor_scalar(buf["v"][:], buf["bn2"][:, 1:2],
                                float(eps), None, ALU.add)
                tr.inc("dve", v.drain(), tag=tag)

            def ln_rstd(v, buf, tag, act_form):
                v.reciprocal(buf["rstd"][:], buf["sq"][:])
                v.drain()
                if act_form:
                    v.tensor_scalar(buf["nmu"][:], buf["bn2"][:, 0:1],
                                    buf["rstd"][:, 0:1], -1.0,
                                    ALU.mult, ALU.mult)
                else:
                    v.tensor_scalar(buf["nmu"][:], buf["bn2"][:, 0:1],
                                    -1.0, None, ALU.mult)
                tr.inc("dve", v.drain(), tag=tag)

            def e_col(h):
                return 1024 * (h // 8) + 512 * (h % 2) + 128 * ((h % 8) // 2)

            def rope(eng, src, m1, dst, cos_b, sin_b):
                swap = src[:].rearrange("p (h t f) -> p h t f",
                                        h=HEADS, t=2, f=32)[:, :, ::-1, :]
                eng.tensor_tensor(
                    m1[:].rearrange("p (h f) -> p h f", h=HEADS),
                    src[:].rearrange("p (h f) -> p h f", h=HEADS),
                    cos_b, ALU.mult)
                eng.tensor_tensor(
                    dst[:].rearrange("p (h t f) -> p h t f", t=2, f=32),
                    swap,
                    sin_b.rearrange("p h (t f) -> p h t f", t=2),
                    ALU.mult)
                return eng.tensor_tensor(dst[:], dst[:], m1[:], ALU.add)

            esA.enter_context(nc.named_scope("phaseA"))
            xnT = [xnT_s, xnT_s2]
            for it in range(-1, NTILE + 2):
                tp = it + 1 if 0 <= it + 1 < NTILE else -1
                tf = it if 0 <= it < NTILE else -1
                tb = it - 1 if 0 <= it - 1 < NTILE else -1
                te = it - 2 if 0 <= it - 2 < NTILE else -1

                @block.sync
                def _(sync, tf=tf):
                    if tf >= 0 and tf + 3 < NTILE:
                        tr.wait(sync, "sync", f"x1add_{tf - 1}")
                        s2, i2 = divmod(tf + 3, NCHUNK)
                        sync.dma_start(out=xbuf[(tf + 3) % 4][:],
                                       in_=x_d[s2, 128 * i2:128 * (i2 + 1), :]
                                       ).then_inc(tr.sems["dx"], 16)
                        tr.cnt["dx"] += 16
                        tr.mark(f"x_{tf + 3}", "dx")

                if te >= 0:
                    @block.scalar
                    def _(a, te=te):
                        tr.wait(a, "act", f"lostats_{te}")
                        a.activation(lo["sq"][:], lo["v"][:], AF.Sqrt)
                        tr.inc("act", a.drain(), tag=f"losqrt_{te}")

                @block.vector
                def _(v, tp=tp, tb=tb, te=te):
                    if tb >= 0 and tb % NCHUNK == 0:
                        tr.wait(v, "dve", f"supd_{tb - 1}")
                        tr.wait(v, "dve", f"qsav_{tb - 1}")
                        ins = v.memset(ssb_s[:], 0.0)
                        tr.inc("dve", ins, tag=f"ssb_{tb}")
                    if te >= 0:
                        tr.wait(v, "dve", f"womm_{te}")
                        tr.wait(v, "dve", "bcast_evac")
                        v.tensor_tensor(y_s[:, 0:512], P3[:],
                                        b1row_bc[:, 0:512], ALU.add)
                        ins = v.tensor_tensor(y_s[:, 512:1024], P4[:],
                                              b1row_bc[:, 512:1024], ALU.add)
                        tr.inc("dve", ins, tag=f"yattn_{te}")
                        ln_stats(v, lo, y_s[:], f"lostats_{te}", EPS)
                    if tb >= 0:
                        tr.wait(v, "dve", f"eT01_{tb}")
                        tr.wait(v, "dve", f"qsav_{tb - 1}")
                        v.tensor_tensor(e_s[:, 0:512], P3[:], mask_sb[:],
                                        ALU.mult)
                        ins = v.tensor_tensor(e_s[:, 512:1024], P4[:],
                                              mask_sb[:], ALU.mult)
                        tr.inc("dve", ins, tag=f"mask0_{tb}")
                        tr.wait(v, "dve", f"eT23_{tb}")
                        v.tensor_tensor(e_s[:, 1024:1536], P3[:], mask_sb[:],
                                        ALU.mult)
                        ins = v.tensor_tensor(e_s[:, 1536:2048], P4[:],
                                              mask_sb[:], ALU.mult)
                        tr.inc("dve", ins, tag=f"mask1_{tb}")
                    if te >= 0:
                        tr.wait(v, "dve", f"losqrt_{te}")
                        ln_rstd(v, lo, f"lorstd_{te}", act_form=False)
                        tr.wait(v, "dve", f"x1add_{te - 1}")
                        v.tensor_scalar(t_s[:], y_s[:], lo["nmu"][:, 0:1],
                                        lo["rstd"][:, 0:1], ALU.add, ALU.mult)
                        tr.inc("dve", v.drain(), tag=f"normts_{te}")
                    if tp >= 0:
                        tr.wait(v, "dve", f"x_{tp}")
                        ln_stats(v, l1, xbuf[tp % 4][:], f"l1stats_{tp}", EPS)
                    if tb >= 0:
                        tr.wait(v, "dve", f"tanh_{tb}")
                        tr.wait(v, "dve", f"qsav_{tb}")
                        tr.wait(v, "dve", f"zppT_t_{tb - 1}")
                        v.scalar_tensor_tensor(z_s[:, 0:512], gate_s[:, 0:512],
                                               1.0, P5[:], ALU.add, ALU.mult)
                        ins = v.scalar_tensor_tensor(z_s[:, 512:1024],
                                                     gate_s[:, 512:1024],
                                                     1.0, P6[:], ALU.add,
                                                     ALU.mult)
                        tr.inc("dve", ins, tag=f"z1_{tb}")
                        ln_stats(v, li, z_s[:], f"listats_{tb}", 4.0 * EPS)
                        tr.wait(v, "dve", f"lisqrt_{tb}")
                        ln_rstd(v, li, f"lirstd_{tb}", act_form=False)
                        ins = v.tensor_scalar(z_s[:], z_s[:],
                                              li["nmu"][:, 0:1],
                                              li["rstd"][:, 0:1],
                                              ALU.add, ALU.mult)
                        tr.inc("dve", ins, tag=f"zpp_{tb}")
                    if tp >= 0:
                        tr.wait(v, "dve", f"l1sqrt_{tp}")
                        ln_rstd(v, l1, f"l1rstd_{tp}", act_form=False)
                        tr.wait(v, "dve", "bcast_mm")
                        tr.wait(v, "dve", f"xnT_t_{tp - 1}")
                        ins = v.tensor_scalar(xn_s[:], xbuf[tp % 4][:],
                                              l1["nmu"][:, 0:1],
                                              l1["rstd"][:, 0:1],
                                              ALU.add, ALU.mult)
                        tr.inc("dve", ins, tag=f"xn_{tp}")

                @block.scalar
                def _(a, tf=tf, tb=tb):
                    if tf >= 0:
                        tr.wait(a, "act", f"qmm_{tf}")
                        tr.wait(a, "act", f"ropeq_{tf - 1}")
                        a.activation(q_tm[:, 0:512], P1[:], AF.Silu)
                        ins = a.activation(q_tm[:, 512:1024], P2[:], AF.Silu)
                        tr.inc("act", ins, tag=f"qsilu_{tf}")
                    if tb >= 0:
                        tr.wait(a, "act", f"g2mm_{tb}")
                        tr.wait(a, "act", f"z1_{tb - 1}")
                        a.activation(gate_s[:, 0:512], P3[:], AF.Tanh,
                                     scale=0.5)
                        ins = a.activation(gate_s[:, 512:1024], P4[:],
                                           AF.Tanh, scale=0.5)
                        tr.inc("act", ins, tag=f"tanh_{tb}")
                    if tf >= 0:
                        tr.wait(a, "act", f"kmm_{tf}")
                        tr.wait(a, "act", f"ropek_{tf - 1}")
                        a.activation(k_tm[:, 0:512], P1[:], AF.Silu)
                        ins = a.activation(k_tm[:, 512:1024], P2[:], AF.Silu)
                        tr.inc("act", ins, tag=f"ksilu_{tf}")

                @block.scalar
                def _(a, tp=tp, tf=tf, tb=tb):
                    if tb >= 0:
                        tr.wait(a, "act", f"listats_{tb}")
                        a.activation(li["sq"][:], li["v"][:], AF.Sqrt)
                        tr.inc("act", a.drain(), tag=f"lisqrt_{tb}")
                    if tp >= 0:
                        tr.wait(a, "act", f"l1stats_{tp}")
                        a.activation(l1["sq"][:], l1["v"][:], AF.Sqrt)
                        tr.inc("act", a.drain(), tag=f"l1sqrt_{tp}")
                    if tf >= 0:
                        tr.wait(a, "act", f"qt_t_{tf}")
                        tr.wait(a, "act", f"qsav_{tf - 1}")
                        ins = a.copy(qt_s[:], PT[:])
                        tr.inc("act", ins, tag=f"qt_e_{tf}")
                    if tb >= 0:
                        tr.wait(a, "act", f"zppT_t_{tb}")
                        tr.wait(a, "act", f"womm_{tb - 1}")
                        ins = a.copy(zppT_s[:], PT[:])
                        tr.inc("act", ins, tag=f"zppT_e_{tb}")
                    if tf >= 0:
                        tr.wait(a, "act", f"vmm_{tf}")
                        tr.wait(a, "act", f"supd_{tf - 1}")
                        a.copy(v_s[:, 0:512], P1[:])
                        ins = a.copy(v_s[:, 512:1024], P2[:])
                        tr.inc("act", ins, tag=f"vevac_{tf}")
                        tr.wait(a, "act", f"kt_t_{tf}")
                        tr.wait(a, "act", f"eT23_{tf - 1}")
                        ins = a.copy(kt_s[:], PT[:])
                        tr.inc("act", ins, tag=f"kt_e_{tf}")
                    if tp >= 0:
                        tr.wait(a, "act", f"xnT_t_{tp}")
                        tr.wait(a, "act", f"g1mm_{tp - 2}")
                        ins = a.copy(xnT[tp % 2][:], PT[:])
                        tr.inc("act", ins, tag=f"xnT_e_{tp}")
                    if tf >= 1 and tf % NCHUNK != 0:
                        tr.wait(a, "act", f"supd_{tf - 1}")
                        tr.wait(a, "act", f"qsav_{tf - 1}")
                        ins = a.copy(ssb_s[:], P0[:])
                        tr.inc("act", ins, tag=f"ssb_{tf}")
                    if tf >= 0:
                        tr.wait(a, "act", f"g1mm_{tf}")
                        tr.wait(a, "act", f"g2mm_{tf - 1}")
                        ins = a.copy(g1t_s[:], P1[0:64, 0:128])
                        tr.inc("act", ins, tag=f"g1evac_{tf}")

                @block.tensor
                def _(pe, tp=tp, tf=tf, tb=tb):
                    if tb >= 0:
                        ib = tb % NCHUNK
                        tr.wait(pe, "pe", f"kt_e_{tb}")
                        tr.wait(pe, "pe", f"ksilu_{tb}")
                        tr.wait(pe, "pe", "bcast_evac")
                        tr.wait(pe, "pe", f"yattn_{tb - 1}")
                        for h in range(8):
                            base = 64 * (h % 2)
                            bank = P3 if h % 2 == 0 else P4
                            col = 128 * ((h % 8) // 2)
                            ins = pe.matmul(
                                bank[:, col:col + 128],
                                kt_s[base:base + 64,
                                     128 * (h // 2):128 * (h // 2) + 128],
                                qt_s[base:base + 64,
                                     128 * (h // 2):128 * (h // 2) + 128],
                                start=(h < 2), stop=(h >= 6),
                                tile_position=(base, 0))
                        tr.inc("pe", ins, tag=f"eT01_{tb}")
                    if tf >= 0:
                        tr.wait(pe, "pe", f"xnT_e_{tf}")
                        tr.wait(pe, "pe", "w_q")
                        tr.wait(pe, "pe", "b1row_evac")
                        tr.wait(pe, "pe", f"g1evac_{tf - 1}")
                        for c in range(8):
                            lhs = xnT[tf % 2][:, 128 * c:128 * (c + 1)]
                            pe.matmul(P1[:], lhs, wq_sb[:, D * c:D * c + 512],
                                      start=(c == 0), stop=(c == 7))
                            ins = pe.matmul(P2[:], lhs,
                                            wq_sb[:, D * c + 512:D * c + 1024],
                                            start=(c == 0), stop=(c == 7))
                        tr.inc("pe", ins, tag=f"qmm_{tf}")
                    if tb >= 0:
                        tr.wait(pe, "pe", f"mask0_{tb}")
                        for hh in range(8):
                            h = 8 + hh
                            base = 64 * (h % 2)
                            bank = P3 if h % 2 == 0 else P4
                            col = 128 * ((h % 8) // 2)
                            ins = pe.matmul(
                                bank[:, col:col + 128],
                                kt_s[base:base + 64,
                                     128 * (h // 2):128 * (h // 2) + 128],
                                qt_s[base:base + 64,
                                     128 * (h // 2):128 * (h // 2) + 128],
                                start=(hh < 2), stop=(hh >= 6),
                                tile_position=(base, 0))
                        tr.inc("pe", ins, tag=f"eT23_{tb}")
                        tr.wait(pe, "pe", f"g1evac_{tb}")
                        tr.wait(pe, "pe", "w_g2")
                        tr.wait(pe, "pe", f"mask1_{tb}")
                        pe.matmul(P3[:], g1t_s[:], wg2_sb[:, 0:512],
                                  start=True, stop=True)
                        ins = pe.matmul(P4[:], g1t_s[:], wg2_sb[:, 512:1024],
                                        start=True, stop=True)
                        tr.inc("pe", ins, tag=f"g2mm_{tb}")
                        tr.wait(pe, "pe", f"ssb_{tb}")
                        tr.wait(pe, "pe", f"mask1_{tb}")
                        tr.wait(pe, "pe", f"vevac_{tb}")
                        tr.wait(pe, "pe", f"z1_{tb - 1}")
                        for h in range(HEADS):
                            base = 64 * (h % 2)
                            bank = P5 if h % 2 == 0 else P6
                            col = 64 * (h // 2)
                            pe.matmul(bank[:, col:col + 64],
                                      qt_s[base:base + 64,
                                           128 * (h // 2):128 * (h // 2) + 128],
                                      ssb_s[base:base + 64, col:col + 64],
                                      start=(h < 2), stop=False,
                                      tile_position=(base, 0))
                        for h in range(HEADS):
                            bank = P5 if h % 2 == 0 else P6
                            col = 64 * (h // 2)
                            ins = pe.matmul(bank[:, col:col + 64],
                                            e_s[:, e_col(h):e_col(h) + 128],
                                            v_s[:, 64 * h:64 * h + 64],
                                            start=False, stop=(h >= 14))
                        tr.inc("pe", ins, tag=f"qsav_{tb}")
                        for h in range(HEADS):
                            base = 64 * (h % 2)
                            col = 64 * (h // 2)
                            ins = pe.matmul(P0[base:base + 64, col:col + 64],
                                            kr_s[:, 64 * h:64 * h + 64],
                                            v_s[:, 64 * h:64 * h + 64],
                                            start=(ib == 0 and h <= 1),
                                            stop=False,
                                            tile_position=(0, base))
                        tr.inc("pe", ins, tag=f"supd_{tb}")
                    if tf >= 0:
                        tr.wait(pe, "pe", "w_k")
                        tr.wait(pe, "pe", f"qsilu_{tf}")
                        for c in range(8):
                            lhs = xnT[tf % 2][:, 128 * c:128 * (c + 1)]
                            pe.matmul(P1[:], lhs, wk_sb[:, D * c:D * c + 512],
                                      start=(c == 0), stop=(c == 7))
                            ins = pe.matmul(P2[:], lhs,
                                            wk_sb[:, D * c + 512:D * c + 1024],
                                            start=(c == 0), stop=(c == 7))
                        tr.inc("pe", ins, tag=f"kmm_{tf}")
                        tr.wait(pe, "pe", "w_v")
                        tr.wait(pe, "pe", f"ksilu_{tf}")
                        for c in range(8):
                            lhs = xnT[tf % 2][:, 128 * c:128 * (c + 1)]
                            pe.matmul(P1[:], lhs, wv_sb[:, D * c:D * c + 512],
                                      start=(c == 0), stop=(c == 7))
                            ins = pe.matmul(P2[:], lhs,
                                            wv_sb[:, D * c + 512:D * c + 1024],
                                            start=(c == 0), stop=(c == 7))
                        tr.inc("pe", ins, tag=f"vmm_{tf}")
                        tr.wait(pe, "pe", f"ropeq_{tf}")
                        tr.wait(pe, "pe", f"xnT_e_{tf}")
                        for c in range(8):
                            ins = pe.transpose(PT[:, 128 * c:128 * (c + 1)],
                                               qr_s[:, 128 * c:128 * (c + 1)],
                                               ident_s[:])
                        tr.inc("pe", ins, tag=f"qt_t_{tf}")
                    if tb >= 0:
                        tr.wait(pe, "pe", f"zpp_{tb}")
                        tr.wait(pe, "pe", f"qt_e_{tf}" if tf >= 0
                                else f"kt_e_{tb}")
                        for c in range(8):
                            ins = pe.transpose(PT[:, 128 * c:128 * (c + 1)],
                                               z_s[:, 128 * c:128 * (c + 1)],
                                               ident_s[:])
                        tr.inc("pe", ins, tag=f"zppT_t_{tb}")
                    if tf >= 0:
                        tr.wait(pe, "pe", f"ropek_{tf}")
                        tr.wait(pe, "pe", f"zppT_e_{tb}" if tb >= 0
                                else f"qt_e_{tf}")
                        for c in range(8):
                            ins = pe.transpose(PT[:, 128 * c:128 * (c + 1)],
                                               kr_s[:, 128 * c:128 * (c + 1)],
                                               ident_s[:])
                        tr.inc("pe", ins, tag=f"kt_t_{tf}")
                    if tb >= 0:
                        tr.wait(pe, "pe", f"zppT_e_{tb}")
                        tr.wait(pe, "pe", f"tanh_{tb}")
                        tr.wait(pe, "pe", "woprime")
                        for c in range(8):
                            lhs = zppT_s[:, 128 * c:128 * (c + 1)]
                            pe.matmul(P3[:], lhs, wo_sb[:, D * c:D * c + 512],
                                      start=(c == 0), stop=(c == 7))
                            ins = pe.matmul(P4[:], lhs,
                                            wo_sb[:, D * c + 512:D * c + 1024],
                                            start=(c == 0), stop=(c == 7))
                        tr.inc("pe", ins, tag=f"womm_{tb}")
                    if tp >= 0:
                        tr.wait(pe, "pe", f"xn_{tp}")
                        tr.wait(pe, "pe", f"kt_e_{tf}" if tf >= 0
                                else "bcast_evac")
                        for c in range(8):
                            ins = pe.transpose(PT[:, 128 * c:128 * (c + 1)],
                                               xn_s[:, 128 * c:128 * (c + 1)],
                                               ident_s[:])
                        tr.inc("pe", ins, tag=f"xnT_t_{tp}")
                    if tf >= 0:
                        tr.wait(pe, "pe", "w_g1")
                        tr.wait(pe, "pe", f"vevac_{tf}")
                        for c in range(8):
                            ins = pe.matmul(P1[0:64, 0:128],
                                            wg1_sb[:, HD * c:HD * (c + 1)],
                                            xnT[tf % 2][:, 128 * c:128 * (c + 1)],
                                            start=(c == 0), stop=(c == 7))
                        tr.inc("pe", ins, tag=f"g1mm_{tf}")

                @block.gpsimd
                def _(gp, tf=tf, te=te):
                    if tf >= 0:
                        cos_sl2 = cos_sb[:, HD * (tf % NCHUNK):
                                         HD * (tf % NCHUNK + 1)]
                        sin_sl2 = sin_sb[:, HD * (tf % NCHUNK):
                                         HD * (tf % NCHUNK + 1)]
                        cb = cos_sl2.unsqueeze(1).broadcast_to((128, HEADS, HD))
                        sb_ = sin_sl2.unsqueeze(1).broadcast_to((128, HEADS, HD))
                        tr.wait(gp, "pool", f"qsilu_{tf}")
                        tr.wait(gp, "pool", f"qt_t_{tf - 1}")
                        ins = rope(gp, q_tm, qm1, qr_s, cb, sb_)
                        tr.inc("pool", ins, tag=f"ropeq_{tf}")
                        tr.wait(gp, "pool", f"ksilu_{tf}")
                        tr.wait(gp, "pool", f"supd_{tf - 1}")
                        ins = rope(gp, k_tm, km1, kr_s, cb, sb_)
                        tr.inc("pool", ins, tag=f"ropek_{tf}")
                    if te >= 0:
                        tr.wait(gp, "pool", f"normts_{te}")
                        tr.wait(gp, "pool", f"x1dma_{te - 1}")
                        ins = gp.tensor_tensor(x1_s[:], xbuf[te % 4][:],
                                               t_s[:], ALU.add)
                        tr.inc("pool", ins, tag=f"x1add_{te}")
                        gp.dma_start(out=x1_d[te], in_=x1_s[:]
                                     ).then_inc(tr.sems["dxo"], 16)
                        tr.cnt["dxo"] += 16
                        tr.mark(f"x1dma_{te}", "dxo")
                        if te % 2 == 0 and te // 2 < 8:
                            c = te // 2
                            gp.dma_start(
                                out=w1_sb[:, WIDTH * c:WIDTH * (c + 1)],
                                in_=w1_d[128 * c:128 * (c + 1), :]
                                ).then_inc(tr.sems["dw"], 16)
                            tr.cnt["dw"] += 16
                            if c == 7:
                                tr.mark("w_w1", "dw")

        esB = contextlib.ExitStack()
        with esB:
            sbB = lambda name, shape, dt=BF16: esB.enter_context(
                nc.sbuf_tensor(name, shape, dt))
            w2_sb = sbB("w2_sb", [128, 32 * D])
            x1b = [sbB(f"x1b{k}", [128, D]) for k in range(8)]
            xn2_s = sbB("xn2_s", [128, D])
            xn2T_s = sbB("xn2T_s", [128, 8 * 512])
            hT_s = sbB("hT_s", [128, 32 * 512])
            y2_s = sbB("y2_s", [128, D], F32)
            x2_s = [sbB("x2_0", [128, D], F32)]
            lnbufB = {}
            for pfx in ("l2", "lf"):
                lnbufB[pfx] = dict(
                    bna=sbB(f"{pfx}_bna", [128, 12], F32),
                    bn2=sbB(f"{pfx}_bn2", [128, 2], F32),
                    v=sbB(f"{pfx}_v", [128, 1], F32),
                    sq=sbB(f"{pfx}_sq", [128, 1], F32),
                    rstd=sbB(f"{pfx}_rstd", [128, 1], F32),
                    nmu=sbB(f"{pfx}_nmu", [128, 1], F32),
                )
            l2, lf = lnbufB["l2"], lnbufB["lf"]

            def ln_statsB(v, buf, src_ap, tag, eps):
                v.bn_stats(buf["bna"][:, 0:6], src_ap[:, 0:512])
                v.bn_stats(buf["bna"][:, 6:12], src_ap[:, 512:1024])
                v.drain()
                v.bn_aggr(buf["bn2"][:], buf["bna"][:])
                v.drain()
                v.tensor_scalar(buf["v"][:], buf["bn2"][:, 1:2],
                                float(eps), None, ALU.add)
                tr.inc("dve", v.drain(), tag=tag)

            @block.gpsimd
            def _(gp):
                tr.wait(gp, "pool", f"womm_{NTILE - 1}")
                tr.wait(gp, "pool", f"normts_{NTILE - 1}")
                tr.wait(gp, "pool", f"zppT_e_{NTILE - 1}")
                tr.wait(gp, "pool", f"x1add_{NTILE - 1}")
                for c in range(32):
                    gp.dma_start(out=w2_sb[:, D * c:D * (c + 1)],
                                 in_=w2_d[128 * c:128 * (c + 1), :]
                                 ).then_inc(tr.sems["dw2"], 16)
                    tr.cnt["dw2"] += 16
                tr.mark("w_w2", "dw2")

            @block.sync
            def _(sync):
                tr.wait(sync, "sync", f"womm_{NTILE - 1}")
                tr.wait(sync, "sync", f"normts_{NTILE - 1}")
                tr.wait(sync, "sync", f"zppT_e_{NTILE - 1}")
                tr.wait(sync, "sync", f"x1add_{NTILE - 1}")
                tr.wait(sync, "sync", f"x1dma_{NTILE - 1}")
                for t in range(8):
                    tr.wait(sync, "sync", f"x1dma_{t}")
                    sync.dma_start(out=x1b[t][:], in_=x1_d[t]
                                   ).then_inc(tr.sems["dx1"], 16)
                    tr.cnt["dx1"] += 16
                    tr.mark(f"x1in_{t}", "dx1")

            xn2T_s2 = sbB("xn2T_s2", [128, 8 * 512])
            xn2T = [xn2T_s, xn2T_s2]

            esB.enter_context(nc.named_scope("phaseB"))
            NG = NTILE // 4
            banks = [P1, P2, P3, P4]
            for g in range(-1, NG + 1):
                gf = g if 0 <= g < NG else -1
                gp = g + 1 if 0 <= g + 1 < NG else -1
                gb = g - 1 if g - 1 >= 0 else -1

                @block.sync
                def _(sync, gf=gf, gp=gp, gb=gb):
                    if gb >= 0:
                        for j in range(4):
                            t = 4 * gb + j
                            tr.wait(sync, "sync", f"x2add_{t}")
                            s2, i2 = divmod(t, NCHUNK)
                            sync.dma_start(
                                out=out_d[s2, 128 * i2:128 * (i2 + 1), :],
                                in_=x2_s[0][:]
                                ).then_inc(tr.sems["dout"], 16)
                            tr.cnt["dout"] += 16
                            tr.mark(f"x2dma_{t}", "dout")
                    if gp >= 2:
                        for j in range(4):
                            t2 = 4 * gp + j
                            tr.wait(sync, "sync", f"x2add_{t2 - 8}")
                            tr.wait(sync, "sync", f"x1dma_{t2}")
                            sync.dma_start(out=x1b[t2 % 8][:], in_=x1_d[t2]
                                           ).then_inc(tr.sems["dx1"], 16)
                            tr.cnt["dx1"] += 16
                            tr.mark(f"x1in_{t2}", "dx1")

                @block.vector
                def _(v, gf=gf, gp=gp):
                    if gp >= 0:
                        for j in range(4):
                            t = 4 * gp + j
                            tr.wait(v, "dve", f"x1in_{t}")
                            tr.wait(v, "dve", f"xn2_{t - 1}")
                            ln_statsB(v, l2, x1b[t % 8][:], f"l2stats_{t}",
                                      EPS)
                            tr.wait(v, "dve", f"l2sqrt_{t}")
                            v.reciprocal(l2["rstd"][:], l2["sq"][:])
                            v.drain()
                            v.tensor_scalar(l2["nmu"][:], l2["bn2"][:, 0:1],
                                            l2["rstd"][:, 0:1], -1.0,
                                            ALU.mult, ALU.mult)
                            tr.inc("dve", v.drain(), tag=f"l2rstd_{t}")
                    if gf >= 0:
                        for j in range(4):
                            t = 4 * gf + j
                            tr.wait(v, "dve", f"ymm_{t}")
                            tr.wait(v, "dve", "bcast_evac")
                            v.tensor_tensor(y2_s[:, 0:512], P5[:],
                                            b2row_bc[:, 0:512], ALU.add)
                            ins = v.tensor_tensor(y2_s[:, 512:1024], P6[:],
                                                  b2row_bc[:, 512:1024],
                                                  ALU.add)
                            tr.inc("dve", ins, tag=f"y2_{t}")
                            ln_statsB(v, lf, y2_s[:], f"lfstats_{t}", EPS)
                            tr.wait(v, "dve", f"lfsqrt_{t}")
                            v.reciprocal(lf["rstd"][:], lf["sq"][:])
                            v.drain()
                            v.tensor_scalar(lf["nmu"][:], lf["bn2"][:, 0:1],
                                            -1.0, None, ALU.mult)
                            v.drain()
                            tr.wait(v, "dve", f"x2dma_{t - 1}")
                            v.tensor_scalar(y2_s[:], y2_s[:],
                                            lf["nmu"][:, 0:1],
                                            lf["rstd"][:, 0:1],
                                            ALU.add, ALU.mult)
                            v.tensor_tensor(x2_s[0][:], y2_s[:],
                                            x1b[t % 8][:], ALU.add)
                            tr.inc("dve", v.drain(), tag=f"x2add_{t}")

                @block.scalar
                def _(a, gf=gf, gp=gp):
                    def relu(c):
                        tr.wait(a, "act", f"hmm_{gf}_{c}")
                        if gf > 0 and c == 0:
                            tr.wait(a, "act", f"ymm_{4 * gf - 1}")
                        ins = a.activation(
                            hT_s[:, 512 * c:512 * (c + 1)],
                            banks[c % 4][:], AF.Relu,
                            bias=b1_sb[:, c:c + 1])
                        tr.inc("act", ins, tag=f"relu_{gf}_{c}")
                    for j in range(4):
                        if gf >= 0:
                            for c in range(8 * j, 8 * j + 4):
                                relu(c)
                        if gp >= 0:
                            t = 4 * gp + j
                            tr.wait(a, "act", f"l2stats_{t}")
                            a.activation(l2["sq"][:], l2["v"][:], AF.Sqrt)
                            tr.inc("act", a.drain(), tag=f"l2sqrt_{t}")
                            tr.wait(a, "act", f"l2rstd_{t}")
                            tr.wait(a, "act", f"xn2T_t_{t - 1}")
                            ins = a.activation(xn2_s[:], x1b[t % 8][:],
                                               AF.Identity,
                                               bias=l2["nmu"][:, 0:1],
                                               scale=l2["rstd"][:, 0:1])
                            tr.inc("act", ins, tag=f"xn2_{t}")
                        if gf >= 0:
                            for c in range(8 * j + 4, 8 * j + 8):
                                relu(c)
                        if gp >= 0:
                            t = 4 * gp + j
                            tr.wait(a, "act", f"xn2T_t_{t}")
                            if gp >= 2:
                                tr.wait(a, "act", f"hmm_{gp - 2}_31")
                            ins = a.copy(
                                xn2T[gp % 2][:].rearrange(
                                    "p (c q) -> p c q", c=8)
                                [:, :, 128 * j:128 * (j + 1)],
                                PT[:].rearrange("p (c q) -> p c q", c=8))
                            tr.inc("act", ins, tag=f"xn2T_e_{t}")
                    if gf >= 0:
                        for j in range(4):
                            t = 4 * gf + j
                            tr.wait(a, "act", f"lfstats_{t}")
                            a.activation(lf["sq"][:], lf["v"][:], AF.Sqrt)
                            tr.inc("act", a.drain(), tag=f"lfsqrt_{t}")

                @block.tensor
                def _(pe, gf=gf, gp=gp):
                    if gf >= 0:
                        tr.wait(pe, "pe", f"xn2T_e_{4 * gf + 3}")
                        tr.wait(pe, "pe", "w_w1")
                        if gf == 0:
                            tr.wait(pe, "pe", f"yattn_{NTILE - 1}")
                            tr.wait(pe, "pe", f"g1evac_{NTILE - 1}")
                            tr.wait(pe, "pe", f"tanh_{NTILE - 1}")
                        for c in range(32):
                            if c >= 4:
                                tr.wait(pe, "pe", f"relu_{gf}_{c - 4}")
                            elif gf > 0:
                                tr.wait(pe, "pe", f"relu_{gf - 1}_{28 + c}")
                            for k in range(8):
                                ins = pe.matmul(
                                    banks[c % 4][:],
                                    w1_sb[:, WIDTH * k + 128 * c:
                                          WIDTH * k + 128 * (c + 1)],
                                    xn2T[gf % 2][:, 512 * k:512 * (k + 1)],
                                    start=(k == 0), stop=(k == 7))
                            tr.inc("pe", ins, tag=f"hmm_{gf}_{c}")
                            if gp >= 0 and c % 8 == 7:
                                j = c // 8
                                t = 4 * gp + j
                                tr.wait(pe, "pe", f"xn2_{t}")
                                tr.wait(pe, "pe", f"xn2T_e_{t - 1}")
                                if t == 0:
                                    tr.wait(pe, "pe", f"zppT_e_{NTILE - 1}")
                                for cc in range(8):
                                    ins = pe.transpose(
                                        PT[:, 128 * cc:128 * (cc + 1)],
                                        xn2_s[:, 128 * cc:128 * (cc + 1)],
                                        ident_s[:])
                                tr.inc("pe", ins, tag=f"xn2T_t_{t}")
                        for j in range(4):
                            t = 4 * gf + j
                            tr.wait(pe, "pe", f"relu_{gf}_31")
                            tr.wait(pe, "pe", "w_w2")
                            if t == 0:
                                tr.wait(pe, "pe", f"z1_{NTILE - 1}")
                            tr.wait(pe, "pe", f"y2_{t - 1}")
                            for c in range(32):
                                lhs = hT_s[:, 512 * c + 128 * j:
                                           512 * c + 128 * (j + 1)]
                                pe.matmul(P5[:], lhs,
                                          w2_sb[:, D * c:D * c + 512],
                                          start=(c == 0), stop=(c == 31))
                                ins = pe.matmul(
                                    P6[:], lhs,
                                    w2_sb[:, D * c + 512:D * c + 1024],
                                    start=(c == 0), stop=(c == 31))
                            tr.inc("pe", ins, tag=f"ymm_{t}")
                    elif gp >= 0:
                        for j in range(4):
                            t = 4 * gp + j
                            tr.wait(pe, "pe", f"xn2_{t}")
                            tr.wait(pe, "pe", f"xn2T_e_{t - 1}")
                            if t == 0:
                                tr.wait(pe, "pe", f"zppT_e_{NTILE - 1}")
                            for cc in range(8):
                                ins = pe.transpose(
                                    PT[:, 128 * cc:128 * (cc + 1)],
                                    xn2_s[:, 128 * cc:128 * (cc + 1)],
                                    ident_s[:])
                            tr.inc("pe", ins, tag=f"xn2T_t_{t}")

            @block.sync
            def _(sync):
                sync.wait_ge(tr.sems["dout"], 16 * NTILE)

    return nc


def build_program(debug_x1=False):
    tr1 = Tracker()
    _emit(tr1, debug_x1)
    tr2 = Tracker(plan=tr1.m)
    return _emit(tr2, debug_x1)




def _host_constants():
    pos = np.arange(SEQ, dtype=np.float64)
    inv = 1.0 / (10000.0 ** (np.arange(0, HD, 2, dtype=np.float64) / HD))
    ang = pos[:, None] * inv[None, :]
    cos = np.concatenate([np.cos(ang), np.cos(ang)], axis=1)
    sin = np.concatenate([np.sin(ang), np.sin(ang)], axis=1)
    sgn = np.ones((1, HD)); sgn[0, :32] = -1.0
    cos_t = cos.astype(ml_dtypes.bfloat16)
    sin_t = (sin * sgn).astype(ml_dtypes.bfloat16)
    mask4 = np.tile(np.triu(np.ones((128, 128), dtype=np.float32)), (1, 4))
    mask4 = mask4.astype(ml_dtypes.bfloat16)
    ident = np.eye(128, dtype=np.float32).astype(ml_dtypes.bfloat16)
    ones = np.ones((1, 128), dtype=np.float32)
    return cos_t, sin_t, mask4, ident, ones


_NC_CACHE = {}


def kernel(L0_attn_x, L0_attn_Wq, L0_attn_Wk, L0_attn_Wv, L0_attn_Wo,
           L0_attn_Wg1, L0_attn_Wg2, L0_attn_ln_w, L0_attn_ln_b,
           L0_ffn_w1, L0_ffn_b1, L0_ffn_w2, L0_ffn_b2):
    x = np.ascontiguousarray(np.asarray(L0_attn_x, dtype=np.float32))
    cos_t, sin_t, mask4, ident, ones = _host_constants()
    if "nc" not in _NC_CACHE:
        _NC_CACHE["nc"] = build_program()
    nc = _NC_CACHE["nc"]
    common = {
        "wq": np.ascontiguousarray(np.asarray(L0_attn_Wq, np.float32)),
        "wk": np.ascontiguousarray(np.asarray(L0_attn_Wk, np.float32)),
        "wv": np.ascontiguousarray(np.asarray(L0_attn_Wv, np.float32)),
        "wo": np.ascontiguousarray(np.asarray(L0_attn_Wo, np.float32)),
        "wg1": np.ascontiguousarray(np.asarray(L0_attn_Wg1, np.float32)),
        "wg2": np.ascontiguousarray(np.asarray(L0_attn_Wg2, np.float32)),
        "lnw": np.ascontiguousarray(np.asarray(L0_attn_ln_w, np.float32)),
        "lnb": np.ascontiguousarray(np.asarray(L0_attn_ln_b, np.float32)),
        "w1": np.ascontiguousarray(np.asarray(L0_ffn_w1, np.float32)),
        "b1": np.ascontiguousarray(np.asarray(L0_ffn_b1, np.float32)),
        "w2": np.ascontiguousarray(np.asarray(L0_ffn_w2, np.float32)),
        "b2": np.ascontiguousarray(np.asarray(L0_ffn_b2, np.float32)),
        "costab": cos_t, "sintab": sin_t, "mask4": mask4,
        "ident": ident, "onesrow": ones,
    }
    in_maps = []
    for c in range(N_CORES):
        m = dict(common)
        m["x"] = np.ascontiguousarray(x[NSEQ * c:NSEQ * (c + 1)])
        in_maps.append(m)
    res = run_bass_kernel_spmd(nc, in_maps, core_ids=list(range(N_CORES)))
    out = np.concatenate([r["out"] for r in res.results], axis=0)
    return out.reshape(B, SEQ, D)
